# version 11
# baseline (speedup 1.0000x reference)
import math
import numpy as np
import concourse.bass as bass
import concourse.mybir as mybir
from concourse.bass_utils import run_bass_kernel_spmd

F32 = mybir.dt.float32
BF16 = mybir.dt.bfloat16
AF = mybir.ActivationFunctionType
ALU = mybir.AluOpType
AX = mybir.AxisListType

H, D, BS = 32, 64, 256
DM = 2048
TK = 1024
FH = 8192
NEG = -1.0e30
GRP = 8192
EPS = 1e-5

_slopes = None


def slopes():
    global _slopes
    if _slopes is None:
        base = 1.0 / 2.0 ** (8.0 / H)
        fac = 1.0 - 0.0 / (H - 1 + 1e-5) + 1e-5
        _slopes = [float(base ** (h + 1) * fac) for h in range(H)]
    return _slopes



def make_plan():
    off = {}
    cur = 0

    def add(key, w):
        nonlocal cur
        off[key] = cur
        cur += w

    for q in range(4):
        for ic in range(16):
            add(("PK", q, ic), 512)
        for ic in range(16):
            add(("PV", q, ic), 512)
    for q in range(4):
        for cs in range(4):
            for ic in range(16):
                add(("Q", q, cs, ic), 128)
        for cs in range(4):
            for ic in range(16):
                add(("K", q, cs, ic), 128)
        for ic in range(16):
            add(("V", q, ic), 512)
    for gq in range(4):
        for ic in range(16):
            add(("gw", gq, ic), 512)
        for dch in range(4):
            for gch in range(4):
                add(("ow", gq, dch, gch), 512)
    for r in range(4):
        for sl in range(16):
            for ic in range(16):
                add(("wg", r, sl, ic), 128)
            for ic in range(16):
                add(("wu", r, sl, ic), 128)
        for dch in range(4):
            for hc in range(16):
                add(("wd", r, dch, hc), 512)
    return off, cur


OFF, STREAM_TOT = make_plan()
RING_GROUPS = []
for _q in range(4):
    RING_GROUPS.append(("PK", _q, 0))
    RING_GROUPS.append(("PV", _q, 0))
for _q in range(4):
    RING_GROUPS.append(("Q", _q, 0, 0))
    RING_GROUPS.append(("K", _q, 0, 0))
    RING_GROUPS.append(("V", _q, 0))
RING_START = {k: OFF[k] for k in RING_GROUPS}



class EngT:
    def __init__(self, sem, step=1):
        self.sem = sem
        self.step = step
        self.ops = []
        self.tick = 0
        self.waited = {}

    def wait(self, other, tick):
        if other is None or tick is None or tick <= 0:
            return
        if self.waited.get(id(other), 0) >= tick:
            return
        self.waited[id(other)] = tick
        self.ops.append(lambda e, o=other, t=tick: e.wait_ge(o.sem, t * o.step))

    def emit(self, fn, inc=True):
        if inc:
            self.tick += 1
            s, st = self.sem, self.step
            self.ops.append(lambda e, f=fn, s=s, st=st: f(e).then_inc(s, st))
            return self.tick
        self.ops.append(fn)
        return None

    def replay(self, e):
        for f in self.ops:
            f(e)



def build_program(taps=()):
    nc = bass.Bass("TRN2", target_bir_lowering=False)
    Xd = nc.dram_tensor("X", [TK, DM], F32, kind="ExternalInput").ap()
    XPd = nc.dram_tensor("XP", [TK, DM], F32, kind="ExternalInput").ap()
    WSd = nc.dram_tensor("WS", [128, STREAM_TOT], F32, kind="ExternalInput").ap()
    MNDd = nc.dram_tensor("MND", [128, 512], F32, kind="ExternalInput").ap()
    KDPd = nc.dram_tensor("KDP", [128, 4096], F32, kind="ExternalInput").ap()
    QDBd = nc.dram_tensor("QDB", [128, 4096], F32, kind="ExternalInput").ap()
    NWd = nc.dram_tensor("NW", [128, 16], F32, kind="ExternalInput").ap()
    OUTd = nc.dram_tensor("OUT", [TK, DM], F32, kind="ExternalOutput").ap()
    tap_d = {}
    for tname, tshape, tdt in [
        ("xn", [128, 16384], BF16), ("attn", [128, 16384], BF16),
        ("resb", [128, 16384], F32), ("S", [128, 2048], F32),
        ("kfm", [128, 4096], BF16), ("vtm", [128, 4096], BF16),
        ("kp", [128, 4096], BF16), ("vp", [128, 4096], BF16),
        ("xnp", [128, 16384], BF16), ("act0", [128, 16384], BF16),
        ("wgu0", [128, 4096], BF16), ("slu0", [128, 1024], BF16),
    ]:
        if tname in taps:
            tap_d[tname] = nc.dram_tensor("tap_" + tname, tshape, tdt,
                                          kind="ExternalOutput").ap()

    ARENA = 53080
    arena = nc.alloc_sbuf_tensor("arena", [128, ARENA], F32)
    base = nc.lookup_mloc(arena).addr

    def sb(name, shape, dtype, col):
        return nc.alloc_sbuf_tensor_at(name, shape, dtype, offset=base + col * 4).ap()

    xn = sb("xn", [128, 16384], BF16, 0)
    res = sb("res", [128, 16384], F32, 8192)
    xnp = sb("xnp", [128, 16384], BF16, 8192)
    mKx = [sb("mK0", [128, 4096], BF16, 8192), sb("mK1", [128, 4096], BF16, 16384)]
    mVx = [sb("mV0", [128, 4096], BF16, 10240), sb("mV1", [128, 4096], BF16, 18432)]
    mQx = [sb("mQ0", [128, 4096], BF16, 12288), sb("mQ1", [128, 4096], BF16, 20480)]
    mKfx = [sb("mKf0", [128, 4096], BF16, 14336), sb("mKf1", [128, 4096], BF16, 22528)]
    xt = [sb("xt0", [128, 2048], F32, 16384), sb("xt1", [128, 2048], F32, 18432)]
    sq = sb("sq", [128, 2048], F32, 20480)
    xnt = sb("xnt", [128, 2048], BF16, 22528)
    s1v = sb("s1v", [128, 1], F32, 53016)
    s2v = sb("s2v", [128, 1], F32, 53024)
    mv = sb("mv", [128, 1], F32, 53032)
    m2v = sb("m2v", [128, 1], F32, 53040)
    v1v = sb("v1v", [128, 1], F32, 53048)
    v2v = sb("v2v", [128, 1], F32, 53056)
    sdv = sb("sdv", [128, 1], F32, 53064)
    attn = sb("attn", [128, 16384], BF16, 24576)
    act = attn
    S = sb("S", [128, 2048], F32, 32768)
    wr = [sb("wr0", [128, GRP], BF16, 34816), sb("wr1", [128, GRP], BF16, 38912)]
    kp = sb("kp", [128, 4096], BF16, 43008)
    vp = sb("vp", [128, 4096], BF16, 45056)
    ptmp = sb("ptmp", [128, 512], BF16, 47104)
    qsc = sb("qsc", [128, 1024], F32, 47104)
    ddT = sb("ddT", [128, 4096], BF16, 43008)
    awb = [sb("awb0", [128, 512], BF16, 48136),
           sb("awb1", [128, 512], BF16, 48392)]
    stg2 = sb("stg2", [128, 2048], F32, 43008)
    gwx = [sb("gw0", [128, 8192], BF16, 36864),
           sb("gw1", [128, 8192], BF16, 32768)]
    owc = [sb("owc0", [128, 2048], BF16, 40960),
           sb("owc1", [128, 2048], BF16, 43008)]
    gfm_all = sb("gfm_all", [128, 4096], BF16, 45056)
    sqB = sb("sqB", [128, 2048], F32, 45056)
    gate_sbx = [sb("gate_sb0", [128, 512], BF16, 47104),
                sb("gate_sb1", [128, 512], BF16, 47360)]
    gtmx = [sb("gtm0", [128, 512], BF16, 47616), sb("gtm1", [128, 512], BF16, 47872)]
    rstdB = sb("rstdB", [128, 8], F32, 48128)
    wgu = [sb("wgu0", [128, 4096], BF16, 32768), sb("wgu1", [128, 4096], BF16, 34816)]
    wd = [sb("wd0", [128, 8192], BF16, 36864), sb("wd1", [128, 8192], BF16, 40960)]
    silu_sb = sb("silu_sb", [128, 1024], BF16, 45056)
    MND = sb("MND", [128, 512], BF16, 48648)
    KDP = sb("KDP", [128, 4096], BF16, 48904)
    QDB = sb("QDB", [128, 4096], BF16, 50952)
    NW = sb("NW", [128, 16], F32, 53000)

    psf = [nc.alloc_psum_tensor("psf%d" % j, [128, 512], F32).ap() for j in range(8)]

    sem_gp = nc.alloc_semaphore("s_gp")
    sem_sy = nc.alloc_semaphore("s_sy")
    sem_pe = nc.alloc_semaphore("s_pe")
    sem_dv = nc.alloc_semaphore("s_dv")
    sem_ac = nc.alloc_semaphore("s_ac")
    GPD = EngT(sem_gp, 16)
    SYD = EngT(sem_sy, 16)
    PE = EngT(sem_pe, 1)
    DVE = EngT(sem_dv, 1)
    ACT = EngT(sem_ac, 1)

    GPD.emit(lambda e: e.dma_start(MND[:, :], MNDd))
    GPD.emit(lambda e: e.dma_start(KDP[:, :], KDPd))
    GPD.emit(lambda e: e.dma_start(QDB[:, :], QDBd))
    t_aux = GPD.emit(lambda e: e.dma_start(NW[:, :], NWd))

    ring_last = {0: None, 1: None}
    ring_tick = {}
    n_loaded = [0]

    def ring_load_next():
        g = n_loaded[0]
        if g >= len(RING_GROUPS):
            return
        n_loaded[0] = g + 1
        buf = g % 2
        c0 = RING_START[RING_GROUPS[g]]
        if ring_last[buf] is not None:
            GPD.wait(*ring_last[buf])
        ring_tick[g] = GPD.emit(
            lambda e, buf=buf, c0=c0: e.dma_start(wr[buf][:, :], WSd[:, c0:c0 + GRP]))

    def ring_tile(gidx, key, width):
        buf = gidx % 2
        rel = OFF[key] - RING_START[RING_GROUPS[gidx]]
        assert 0 <= rel and rel + width <= GRP
        return wr[buf][:, rel:rel + width]

    ring_load_next()
    ring_load_next()

    def MMf(out_ap, l_ap, r_ap, start, stop, tp=None):
        return lambda e: e.matmul(out_ap, l_ap, r_ap, start=start, stop=stop,
                                  tile_position=tp)

    ticks = {}
    tap_store = [0]

    def tap_now(name, src, stage=None):
        if name not in tap_d:
            return
        dap = tap_d[name]
        SYD.wait(DVE, DVE.tick)
        SYD.wait(ACT, ACT.tick)
        SYD.wait(PE, PE.tick)
        SYD.wait(GPD, GPD.tick)
        tap_store[0] = SYD.emit(lambda e, dap=dap, src=src: e.dma_start(dap, src[:, :]))
        DVE.wait(SYD, tap_store[0])
        ACT.wait(SYD, tap_store[0])
        GPD.wait(SYD, tap_store[0])
        PE.wait(SYD, tap_store[0])

    xt_free = {0: (None, None), 1: (None, None)}
    ln_state = {"iter": 0, "sq_free": None, "s_free": None, "xnt_rd": 0}
    psb_free = {0: None, 1: None}

    def ln_pre(part, tc):
        src = Xd if part == 0 else XPd
        b = ln_state["iter"] % 2
        ln_state["iter"] += 1
        dvf, acf = xt_free[b]
        SYD.wait(DVE, dvf)
        SYD.wait(ACT, acf)
        tld = SYD.emit(lambda e, b=b, src=src, tc=tc: e.dma_start(
            xt[b][:, :], src[tc * 128:(tc + 1) * 128, :]))
        ACT.wait(SYD, tld)
        if ln_state["s_free"]:
            ACT.wait(DVE, ln_state["s_free"])
        if ln_state["sq_free"] is not None:
            ACT.wait(DVE, ln_state["sq_free"])
        t_s1 = ACT.emit(lambda e, b=b: e.activation(sq[:, :], xt[b][:, :], AF.Copy,
                                                    accum_out=s1v[:, :]))
        t_sq = ACT.emit(lambda e, b=b: e.activation(sq[:, :], xt[b][:, :], AF.Square,
                                                    accum_out=s2v[:, :]))
        DVE.wait(ACT, t_sq)
        t3 = DVE.emit(lambda e: e.tensor_scalar(mv[:, :], s1v[:, :], 1.0 / DM, None, ALU.mult))
        t4 = DVE.emit(lambda e: e.tensor_scalar(v1v[:, :], s2v[:, :], 1.0 / DM, EPS, ALU.mult, ALU.add))
        ln_state["s_free"] = t4
        ln_state["sq_free"] = t4
        DVE.wait(DVE, t3)
        DVE.wait(DVE, t4)
        DVE.emit(lambda e: e.tensor_tensor(m2v[:, :], mv[:, :], mv[:, :], ALU.mult))
        DVE.wait(DVE, DVE.tick)
        t6 = DVE.emit(lambda e: e.tensor_tensor(v2v[:, :], v1v[:, :], m2v[:, :], ALU.subtract))
        ACT.wait(DVE, t6)
        t7 = ACT.emit(lambda e: e.activation(sdv[:, :], v2v[:, :], AF.Sqrt))
        DVE.wait(ACT, t7)
        t8 = DVE.emit(lambda e: e.reciprocal(s1v[:, :], sdv[:, :]))
        DVE.wait(DVE, t8)
        DVE.wait(SYD, ln_state["xnt_rd"])
        t9 = DVE.emit(lambda e, b=b: e.tensor_scalar(
            xnt[:, :], xt[b][:, :], mv[:, :], s1v[:, :], ALU.subtract, ALU.mult))
        xt_free[b] = (t9, t_sq)
        return (part, tc, t9)

    def ln_post(st):
        part, tc, t9 = st
        dst = xn if part == 0 else xnp
        SYD.wait(DVE, t9)
        d3 = dst.rearrange("p (ic t) -> p ic t", t=1024)[:, :, tc * 128:(tc + 1) * 128]
        s3 = xnt.rearrange("p (ic t) -> p ic t", t=128)
        ttr = SYD.emit(lambda e, d3=d3, s3=s3: e.dma_start(d3, s3, transpose=True))
        ln_state["xnt_rd"] = ttr
        return ttr

    xnp_tick = {}
    _lnq = None
    for _tc in range(8):
        _st = ln_pre(1, _tc)
        if _lnq is not None:
            xnp_tick[_lnq[1]] = ln_post(_lnq)
        _lnq = _st
    xnp_tick[_lnq[1]] = ln_post(_lnq)
    ticks["xnp"] = max(xnp_tick.values())
    ticks["xn"] = None

    t_ms = DVE.emit(lambda e: e.memset(S[:, :], 0.0))
    ticks["S0"] = t_ms
    bankf = {j: None for j in range(8)}
    sltK = slopes()
    bd = [math.exp(-sv * BS) for sv in sltK]
    kp_t = {}
    vp_t = {}
    ptmp_free = None
    stt_prev = {}
    sbf_t = {}
    gidx = [0]
    q_t = [{}, {}]
    kf_t = [{}, {}]
    ktm_t = [{}, {}]
    v_t = [{}, {}]
    ddT_t = {}
    mask_last = [0]
    po_free = {0: None, 1: None}
    aw_free = {}
    awb_rd = {}
    qsc_t_ref = [0]
    scan_state = {"pe": 0, "dve": 0}
    attn_ticks = []

    def proj_gen(q):
        si = q % 2
        mQ, mKf, mV, mK = mQx[si], mKfx[si], mVx[si], mKx[si]
        q_t[si].clear(); kf_t[si].clear(); ktm_t[si].clear(); v_t[si].clear()
        war = (DVE, scan_state["dve"]) if scan_state["dve"] else None
        for cs in range(4):
            for th in range(2):
                bank = (cs * 2 + th) % 2
                g = gidx[0]
                PE.wait(GPD, ring_tick[g])
                if bankf[bank]:
                    PE.wait(*bankf[bank])
                PE.wait(SYD, ticks["xn"])
                tmm = None
                for ic in range(16):
                    fn = MMf(psf[bank][:, :], ring_tile(g, ("Q", q, cs, ic), 128),
                             xn[:, ic * 1024 + th * 512: ic * 1024 + (th + 1) * 512],
                             ic == 0, ic == 15)
                    tmm = PE.emit(fn) if ic == 15 else PE.emit(fn, inc=False)
                ring_last[g % 2] = (PE, tmm)
                ACT.wait(PE, tmm)
                if war:
                    ACT.wait(*war)
                tq = ACT.emit(lambda e, bank=bank, cs=cs, th=th, mQ=mQ: e.activation(
                    mQ[:, cs * 1024 + th * 512: cs * 1024 + (th + 1) * 512],
                    psf[bank][:, :], AF.Silu))
                bankf[bank] = (ACT, tq)
                q_t[si][(cs, th)] = tq
                yield
        ring_load_next()
        gidx[0] += 1
        for cs in range(4):
            for th in range(2):
                bank = (cs * 2 + th) % 2
                g = gidx[0]
                PE.wait(GPD, ring_tick[g])
                if bankf[bank]:
                    PE.wait(*bankf[bank])
                tmm = None
                for ic in range(16):
                    fn = MMf(psf[bank][:, :], ring_tile(g, ("K", q, cs, ic), 128),
                             xn[:, ic * 1024 + th * 512: ic * 1024 + (th + 1) * 512],
                             ic == 0, ic == 15)
                    tmm = PE.emit(fn) if ic == 15 else PE.emit(fn, inc=False)
                ring_last[g % 2] = (PE, tmm)
                ACT.wait(PE, tmm)
                if war:
                    ACT.wait(*war)
                tk = ACT.emit(lambda e, bank=bank, cs=cs, th=th, mKf=mKf: e.activation(
                    mKf[:, cs * 1024 + th * 512: cs * 1024 + (th + 1) * 512],
                    psf[bank][:, :], AF.Silu))
                bankf[bank] = (ACT, tk)
                kf_t[si][(cs, th)] = tk
                yield
        ring_load_next()
        gidx[0] += 1
        for tc in range(8):
            bank = tc % 2
            g = gidx[0]
            PE.wait(GPD, ring_tick[g])
            if bankf[bank]:
                PE.wait(*bankf[bank])
            tmm = None
            for ic in range(16):
                fn = MMf(psf[bank][:, :],
                         xn[:, ic * 1024 + tc * 128: ic * 1024 + (tc + 1) * 128],
                         ring_tile(g, ("V", q, ic), 512), ic == 0, ic == 15)
                tmm = PE.emit(fn) if ic == 15 else PE.emit(fn, inc=False)
            ring_last[g % 2] = (PE, tmm)
            ACT.wait(PE, tmm)
            if war:
                ACT.wait(*war)
            tv = ACT.emit(lambda e, bank=bank, tc=tc, mV=mV: e.activation(
                mV[:, tc * 512:(tc + 1) * 512], psf[bank][:, :], AF.Silu))
            bankf[bank] = (ACT, tv)
            v_t[si][tc] = tv
            yield
        ring_load_next()
        gidx[0] += 1
        ttrs = None
        for cs in range(4):
            SYD.wait(ACT, kf_t[si][(cs, 0)])
            SYD.wait(ACT, kf_t[si][(cs, 1)])
            if war:
                SYD.wait(*war)
            s3 = mKf[:, cs * 1024:(cs + 1) * 1024].rearrange("p (tc t) -> p tc t", t=128)
            d3 = mK.rearrange("p (tc c) -> p tc c", c=512)[:, :, cs * 128:(cs + 1) * 128]
            ttrs = SYD.emit(lambda e, s3=s3, d3=d3: e.dma_start(d3, s3, transpose=True))
        for tc in range(8):
            DVE.wait(SYD, ttrs)
            if war:
                DVE.wait(*war)
            tev = DVE.emit(lambda e, tc=tc, q=q, mK=mK: e.tensor_tensor(
                mK[:, tc * 512:(tc + 1) * 512],
                mK[:, tc * 512:(tc + 1) * 512],
                KDP[:, (tc % 2) * 2048 + q * 512:(tc % 2) * 2048 + (q + 1) * 512],
                ALU.mult))
            for cs in range(4):
                ktm_t[si][(cs, tc)] = tev
            if tc % 2 == 1:
                yield

    def prefix_scan(qp, gen):
        for lh in range(8):
            h = 8 * qp + lh
            rb = (lh % 2) * 64
            for blk in range(4):
                if bankf[5]:
                    PE.wait(*bankf[5])
                tC = None
                for jc in range(2):
                    tcj = blk * 2 + jc
                    PE.wait(DVE, kp_t[tcj])
                    PE.wait(ACT, vp_t[tcj])
                    fn = MMf(psf[5][rb:rb + 64, 0:64],
                             kp[:, tcj * 512 + lh * 64: tcj * 512 + (lh + 1) * 64],
                             vp[:, tcj * 512 + lh * 64: tcj * 512 + (lh + 1) * 64],
                             jc == 0, jc == 1, tp=(0, rb))
                    tC = PE.emit(fn) if jc == 1 else PE.emit(fn, inc=False)
                DVE.wait(PE, tC)
                DVE.wait(DVE, stt_prev.get(h))
                DVE.wait(DVE, ticks["S0"])
                tst = DVE.emit(lambda e, h=h, rb=rb, bdh=bd[h]: e.scalar_tensor_tensor(
                    S[rb:rb + 64, h * 64:(h + 1) * 64], S[rb:rb + 64, h * 64:(h + 1) * 64],
                    bdh, psf[5][rb:rb + 64, 0:64], ALU.mult, ALU.add))
                stt_prev[h] = tst
                bankf[5] = (DVE, tst)
                if gen is not None:
                    next(gen, None)

    def main_scan(q, gen):
        si = q % 2
        mQ, mKf, mV, mK = mQx[si], mKfx[si], mVx[si], mKx[si]
        for lh in range(8):
            ACT.wait(GPD, t_aux)
            ACT.wait(PE, p1_pe_done)
            ACT.wait(DVE, p1_dve_done)
            ACT.wait(DVE, mask_last[0])
            ddT_t[lh] = ACT.emit(lambda e, lh=lh, sv=sltK[8 * q + lh]: e.activation(
                ddT[:, lh * 512:(lh + 1) * 512], MND[:, :], AF.Exp, scale=sv))
        PE.wait(ACT, v_t[si][7])
        for _bk in (2, 3, 4, 5, 6, 7):
            if bankf[_bk]:
                PE.wait(*bankf[_bk])
                bankf[_bk] = None
        units = [(lh, blk) for lh in range(8) for blk in range(4)]

        def emit_aw(i):
            lh, blk = units[i]
            cs = lh // 2
            rb = (lh % 2) * 64
            ab = i % 2
            awbank = psf[2] if ab == 0 else psf[4]
            if aw_free.get(ab):
                PE.wait(DVE, aw_free[ab])
            PE.wait(ACT, ddT_t[lh])
            tA = None
            for jc in range(2):
                tcj = blk * 2 + jc
                fn = MMf(awbank[:, jc * 256:(jc + 1) * 256],
                         mKf[rb:rb + 64, cs * 1024 + tcj * 128: cs * 1024 + (tcj + 1) * 128],
                         mQ[rb:rb + 64, cs * 1024 + blk * 256: cs * 1024 + (blk + 1) * 256],
                         True, True)
                tA = PE.emit(fn) if jc == 1 else PE.emit(fn, inc=False)
            DVE.wait(PE, tA)
            DVE.wait(PE, awb_rd.get(ab, 0))
            tM = DVE.emit(lambda e, lh=lh, ab=ab, awbank=awbank: e.tensor_tensor(
                awb[ab][:, :], awbank[:, :], ddT[:, lh * 512:(lh + 1) * 512], ALU.mult))
            aw_free[ab] = tM
            mask_last[0] = tM
            return tM

        def emit_qsc(cs, t_prev_rd):
            DVE.wait(PE, t_prev_rd)
            for blk in range(4):
                DVE.wait(ACT, q_t[si][(cs, 0)])
                DVE.wait(ACT, q_t[si][(cs, 1)])
                qsc_t_ref[0] = DVE.emit(lambda e, cs=cs, blk=blk, q=q, mQ=mQ: e.tensor_tensor(
                    qsc[:, blk * 256:(blk + 1) * 256],
                    mQ[:, cs * 1024 + blk * 256: cs * 1024 + (blk + 1) * 256],
                    QDB[:, (q * 4 + cs) * 256:(q * 4 + cs + 1) * 256], ALU.mult))

        emit_qsc(0, 0)
        tmask = {0: emit_aw(0)}
        tO_prev = 0
        for i in range(32):
            lh, blk = units[i]
            cs = lh // 2
            rb = (lh % 2) * 64
            h = 8 * q + lh
            ab = i % 2
            if i % 8 == 0 and i > 0:
                emit_qsc(cs, tO_prev)
            if i + 1 < 32:
                tmask[i + 1] = emit_aw(i + 1)
            if gen is not None:
                next(gen, None)
            poh = lh % 2
            po = psf[3] if poh == 0 else psf[7]
            PE.wait(DVE, tmask[i])
            PE.wait(DVE, qsc_t_ref[0])
            if blk == 0 and po_free[poh]:
                PE.wait(ACT, po_free[poh])
            PE.wait(DVE, stt_prev.get(h))
            tO = None
            for ich in range(2):
                scol = (blk * 2 + ich) * 64
                PE.emit(MMf(po[:, scol:scol + 64],
                            awb[ab][:, ich * 128:(ich + 1) * 128],
                            mV[:, (blk * 2) * 512 + lh * 64: (blk * 2) * 512 + (lh + 1) * 64],
                            True, False), inc=False)
                PE.emit(MMf(po[:, scol:scol + 64],
                            awb[ab][:, 256 + ich * 128: 256 + (ich + 1) * 128],
                            mV[:, (blk * 2 + 1) * 512 + lh * 64: (blk * 2 + 1) * 512 + (lh + 1) * 64],
                            False, False), inc=False)
                fn = MMf(po[:, scol:scol + 64],
                         qsc[rb:rb + 64, blk * 256 + ich * 128: blk * 256 + (ich + 1) * 128],
                         S[rb:rb + 64, h * 64:(h + 1) * 64], False, True)
                tO = PE.emit(fn)
            awb_rd[ab] = tO
            tO_prev = tO
            cb = 5 if i % 2 == 0 else 6
            if bankf[cb]:
                PE.wait(*bankf[cb])
            tC = None
            for jc in range(2):
                tcj = blk * 2 + jc
                PE.wait(DVE, ktm_t[si][(cs, tcj)])
                PE.wait(ACT, v_t[si][tcj])
                fn = MMf(psf[cb][rb:rb + 64, 0:64],
                         mK[:, tcj * 512 + lh * 64: tcj * 512 + (lh + 1) * 64],
                         mV[:, tcj * 512 + lh * 64: tcj * 512 + (lh + 1) * 64],
                         jc == 0, jc == 1, tp=(0, rb))
                tC = PE.emit(fn) if jc == 1 else PE.emit(fn, inc=False)
            DVE.wait(PE, tC)
            DVE.wait(DVE, stt_prev.get(h))
            tst = DVE.emit(lambda e, h=h, rb=rb, cb=cb, bdh=bd[h]: e.scalar_tensor_tensor(
                S[rb:rb + 64, h * 64:(h + 1) * 64], S[rb:rb + 64, h * 64:(h + 1) * 64],
                bdh, psf[cb][rb:rb + 64, 0:64], ALU.mult, ALU.add))
            stt_prev[h] = tst
            bankf[cb] = (DVE, tst)
            if blk == 3:
                ACT.wait(PE, tO)
                a3 = attn.rearrange("p (a b) -> p a b", b=2048)
                p3 = po.rearrange("p (a b) -> p a b", b=64)
                c0 = q * 512 + lh * 64
                tev = ACT.emit(lambda e, a3=a3, p3=p3, c0=c0: e.activation(
                    a3[:, :, c0:c0 + 64], p3[:, :, :], AF.Copy))
                po_free[poh] = tev
                attn_ticks.append(tev)


    for qp in range(4):
        for kind in ("PK", "PV"):
            ln_main = qp * 2 + (0 if kind == "PK" else 1)
            _mst = ln_pre(0, ln_main)
            for tc in range(8):
                bank = (qp * 8 + tc) % 2
                g = gidx[0]
                PE.wait(GPD, ring_tick[g])
                if bankf[bank]:
                    PE.wait(*bankf[bank])
                PE.wait(SYD, ticks["xnp"])
                tmm = None
                for ic in range(16):
                    fn = MMf(psf[bank][:, :],
                             xnp[:, ic * 1024 + tc * 128: ic * 1024 + (tc + 1) * 128],
                             ring_tile(g, (kind, qp, ic), 512), ic == 0, ic == 15)
                    tmm = PE.emit(fn) if ic == 15 else PE.emit(fn, inc=False)
                ring_last[g % 2] = (PE, tmm)
                ACT.wait(PE, tmm)
                if kind == "PK":
                    if ptmp_free:
                        ACT.wait(DVE, ptmp_free)
                    tsl = ACT.emit(lambda e, bank=bank: e.activation(
                        ptmp[:, :], psf[bank][:, :], AF.Silu))
                    bankf[bank] = (ACT, tsl)
                    DVE.wait(ACT, tsl)
                    tkp = DVE.emit(lambda e, tc=tc, qp=qp: e.tensor_tensor(
                        kp[:, tc * 512:(tc + 1) * 512], ptmp[:, :],
                        KDP[:, (tc % 2) * 2048 + qp * 512:(tc % 2) * 2048 + (qp + 1) * 512],
                        ALU.mult))
                    ptmp_free = tkp
                    kp_t[tc] = tkp
                else:
                    tsl = ACT.emit(lambda e, bank=bank, tc=tc: e.activation(
                        vp[:, tc * 512:(tc + 1) * 512], psf[bank][:, :], AF.Silu))
                    bankf[bank] = (ACT, tsl)
                    vp_t[tc] = tsl
            ring_load_next()
            gidx[0] += 1
            _xn_syd = ln_post(_mst)
            if ln_main == 7:
                ticks["xn"] = _xn_syd
        if qp == 3:
            gen0 = proj_gen(0)
            prefix_scan(qp, gen0)
            for _ in gen0:
                pass
        else:
            prefix_scan(qp, None)

    tap_now("S", S, sq)
    tap_now("kp", kp, sq)
    tap_now("vp", vp, sq)
    tap_now("xn", xn, sq)
    p1_pe_done = PE.tick
    p1_dve_done = DVE.tick

    for q in range(4):
        gen_next = proj_gen(q + 1) if q < 3 else None
        main_scan(q, gen_next)
        if gen_next is not None:
            for _ in gen_next:
                pass
        if aw_free.get(0):
            bankf[2] = (DVE, aw_free[0])
        if aw_free.get(1):
            bankf[4] = (DVE, aw_free[1])
        if po_free[0]:
            bankf[3] = (ACT, po_free[0])
        if po_free[1]:
            bankf[7] = (ACT, po_free[1])
        scan_state["pe"] = PE.tick
        scan_state["dve"] = DVE.tick
    scan_pe_done = PE.tick
    scan_dve_done = DVE.tick
    ticks["attn"] = max(attn_ticks)
    tap_now("attn", attn, stg2)
    tap_now("kfm", mKfx[1], stg2)
    tap_now("vtm", mVx[1], stg2)
    scan_dve_done = DVE.tick

    for _j in (2, 4):
        bankf[_j] = (DVE, scan_dve_done)
    sqB_free = None
    rstd_t = {}
    ACT.wait(DVE, scan_dve_done)
    ACT.wait(PE, scan_pe_done)
    for tc in range(8):
        ACT.wait(ACT, ticks["attn"])
        if sqB_free:
            ACT.wait(DVE, sqB_free)
        tsq = ACT.emit(lambda e, tc=tc: e.activation(
            sqB[:, :], attn[:, tc * 2048:(tc + 1) * 2048], AF.Square))
        DVE.wait(ACT, tsq)
        tr = DVE.emit(lambda e: e.tensor_reduce(s2v[:, :], sqB[:, :], AX.X, ALU.add))
        sqB_free = tr
        DVE.wait(DVE, tr)
        tv_ = DVE.emit(lambda e: e.tensor_scalar(
            v1v[:, :], s2v[:, :], 1.0 / DM, EPS, ALU.mult, ALU.add))
        ACT.wait(DVE, tv_)
        ts_ = ACT.emit(lambda e: e.activation(sdv[:, :], v1v[:, :], AF.Sqrt))
        DVE.wait(ACT, ts_)
        rstd_t[tc] = DVE.emit(lambda e, tc=tc: e.reciprocal(rstdB[:, tc:tc + 1], sdv[:, :]))
    xload = {}
    SYD.wait(SYD, tap_store[0])
    SYD.wait(PE, scan_pe_done)
    SYD.wait(DVE, scan_dve_done)
    for tc in range(8):
        xload[tc] = SYD.emit(lambda e, tc=tc: e.dma_start(
            res[:, tc * 2048:(tc + 1) * 2048], Xd[tc * 128:(tc + 1) * 128, :]))
    res_last = {}
    gw_war = {0: None, 1: None}
    ow_war = {0: None, 1: None}
    ow_tick = {}
    gate_free = {0: None, 1: None}
    gtm_rd = {0: None, 1: None}
    gfm_wr = 0
    gfm_rd = 0
    gw_tick = {}

    def load_gw(gq):
        b = gq % 2
        if gw_war[b]:
            GPD.wait(PE, gw_war[b])
        elif b == 1:
            GPD.wait(PE, scan_pe_done)
            GPD.wait(DVE, scan_dve_done)
        else:
            GPD.wait(PE, ring_last[0][1] if ring_last[0] else 0)
            GPD.wait(PE, ring_last[1][1] if ring_last[1] else 0)
        gw_tick[gq] = GPD.emit(lambda e, gq=gq, b=b: e.dma_start(
            gwx[b][:, :], WSd[:, OFF[("gw", gq, 0)]: OFF[("gw", gq, 0)] + GRP]))

    def load_ow(gq, dch):
        i = gq * 4 + dch
        b = i % 2
        if ow_war[b]:
            GPD.wait(PE, ow_war[b])
        else:
            GPD.wait(DVE, scan_dve_done)
        c0 = OFF[("ow", gq, dch, 0)]
        ow_tick[i] = GPD.emit(lambda e, c0=c0, b=b: e.dma_start(
            owc[b][:, :], WSd[:, c0:c0 + 2048]))

    load_gw(0)
    for gq in range(4):
        if gq + 1 < 4:
            load_gw(gq + 1)
        tg_arr = {}

        def gate_mms(tc, gq=gq):
            bank = tc % 2
            PE.wait(GPD, gw_tick[gq])
            if bankf[bank]:
                PE.wait(*bankf[bank])
            PE.wait(SYD, ticks["xn"])
            tg = None
            for ic in range(16):
                fn = MMf(psf[bank][:, :],
                         xn[:, ic * 1024 + tc * 128: ic * 1024 + (tc + 1) * 128],
                         gwx[gq % 2][:, ic * 512:(ic + 1) * 512], ic == 0, ic == 15)
                tg = PE.emit(fn) if ic == 15 else PE.emit(fn, inc=False)
            tg_arr[tc] = tg

        gate_mms(0)
        for tc in range(8):
            if tc + 1 < 8:
                gate_mms(tc + 1)
            gb = tc % 2
            ACT.wait(PE, tg_arr[tc])
            ACT.wait(DVE, gate_free[gb])
            tsg = ACT.emit(lambda e, gb=gb, bank=tc % 2: e.activation(
                gate_sbx[gb][:, :], psf[bank][:, :], AF.Sigmoid))
            bankf[tc % 2] = (ACT, tsg)
            DVE.wait(ACT, tsg)
            DVE.wait(DVE, rstd_t[tc])
            DVE.wait(SYD, gtm_rd[gb])
            tgt = DVE.emit(lambda e, tc=tc, gq=gq, gb=gb: e.scalar_tensor_tensor(
                gtmx[gb][:, :], attn[:, tc * 2048 + gq * 512: tc * 2048 + (gq + 1) * 512],
                rstdB[:, tc:tc + 1], gate_sbx[gb][:, :], ALU.mult, ALU.mult))
            gate_free[gb] = tgt
            SYD.wait(DVE, tgt)
            SYD.wait(DVE, gfm_wr)
            SYD.wait(PE, gfm_rd)
            s3 = gtmx[gb].rearrange("p (g t) -> p g t", t=128)
            d3 = gfm_all[:, tc * 512:(tc + 1) * 512].rearrange("p (g t) -> p g t", t=128)
            ttp = SYD.emit(lambda e, s3=s3, d3=d3: e.dma_start(d3, s3, transpose=True))
            gtm_rd[gb] = ttp
            tfm = None
            for gch in range(4):
                DVE.wait(SYD, ttp)
                tfm = DVE.emit(lambda e, gch=gch, gq=gq, tc=tc: e.tensor_scalar(
                    gfm_all[:, tc * 512 + gch * 128: tc * 512 + (gch + 1) * 128],
                    gfm_all[:, tc * 512 + gch * 128: tc * 512 + (gch + 1) * 128],
                    NW[:, gq * 4 + gch: gq * 4 + gch + 1], None, ALU.mult))
            gfm_wr = tfm
        load_ow(gq, 0)
        load_ow(gq, 1)
        tpo_last = None
        for dch in range(4):
            b = (gq * 4 + dch) % 2
            for tc in range(8):
                bo = 2 + tc % 2
                if bankf[bo]:
                    PE.wait(*bankf[bo])
                PE.wait(DVE, gfm_wr)
                PE.wait(GPD, ow_tick[gq * 4 + dch])
                tpo = None
                for gch in range(4):
                    fn = MMf(psf[bo][:, :],
                             gfm_all[:, tc * 512 + gch * 128: tc * 512 + (gch + 1) * 128],
                             owc[b][:, gch * 512:(gch + 1) * 512],
                             gch == 0, gch == 3)
                    tpo = PE.emit(fn) if gch == 3 else PE.emit(fn, inc=False)
                DVE.wait(PE, tpo)
                DVE.wait(SYD, xload[tc])
                tra = DVE.emit(lambda e, bo=bo, tc=tc, dch=dch: e.tensor_tensor(
                    res[:, tc * 2048 + dch * 512: tc * 2048 + (dch + 1) * 512],
                    psf[bo][:, :],
                    res[:, tc * 2048 + dch * 512: tc * 2048 + (dch + 1) * 512],
                    ALU.add))
                bankf[bo] = (DVE, tra)
                res_last[tc] = tra
                tpo_last = tpo
            ow_war[b] = tpo_last
            if dch + 2 < 4:
                load_ow(gq, dch + 2)
        gw_war[gq % 2] = tg_arr[7]
        gfm_rd = PE.tick
    gwow_war = PE.tick

    tap_now("resb", res, stg2)

    p3_dve_done = DVE.tick
    p3_act_done = ACT.tick
    wgu_last = {0: None, 1: None}
    wd_last = {0: None, 1: None}
    wgu_tick = {}
    wd_tick = {}
    act_war = (ACT, p3_act_done)
    act_dve_war = (DVE, p3_dve_done)

    def load_wgu(gsl):
        r, sl = gsl // 16, gsl % 16
        b = gsl % 2
        if wgu_last[b]:
            GPD.wait(PE, wgu_last[b])
        else:
            GPD.wait(PE, gwow_war)
        wgu_tick[gsl] = GPD.emit(lambda e, r=r, sl=sl, b=b: e.dma_start(
            wgu[b][:, :], WSd[:, OFF[("wg", r, sl, 0)]: OFF[("wg", r, sl, 0)] + 4096]))

    def load_wd(gdc):
        r, dch = gdc // 4, gdc % 4
        b = gdc % 2
        if wd_last[b]:
            GPD.wait(PE, wd_last[b])
        else:
            GPD.wait(PE, gwow_war)
            GPD.wait(DVE, p3_dve_done)
            GPD.wait(ACT, p3_act_done)
        wd_tick[gdc] = GPD.emit(lambda e, r=r, dch=dch, b=b: e.dma_start(
            wd[b][:, :], WSd[:, OFF[("wd", r, dch, 0)]: OFF[("wd", r, dch, 0)] + GRP]))

    load_wgu(0)
    load_wgu(1)
    tap_now("wgu0", wgu[0])
    act_t = {}
    silu_free = None
    for r in range(4):
        for sl in range(16):
            gsl = r * 16 + sl
            tum = {}
            tgm = {}
            for th in range(2):
                bg = th
                bu = 2 + th
                PE.wait(GPD, wgu_tick[gsl])
                if bankf[bg]:
                    PE.wait(*bankf[bg])
                t_ = None
                for ic in range(16):
                    fn = MMf(psf[bg][:, :], wgu[gsl % 2][:, ic * 128:(ic + 1) * 128],
                             xn[:, ic * 1024 + th * 512: ic * 1024 + (th + 1) * 512],
                             ic == 0, ic == 15)
                    t_ = PE.emit(fn) if ic == 15 else PE.emit(fn, inc=False)
                tgm[th] = t_
                if bankf[bu]:
                    PE.wait(*bankf[bu])
                t_ = None
                for ic in range(16):
                    fn = MMf(psf[bu][:, :], wgu[gsl % 2][:, 2048 + ic * 128: 2048 + (ic + 1) * 128],
                             xn[:, ic * 1024 + th * 512: ic * 1024 + (th + 1) * 512],
                             ic == 0, ic == 15)
                    t_ = PE.emit(fn) if ic == 15 else PE.emit(fn, inc=False)
                tum[th] = t_
            wgu_last[gsl % 2] = PE.tick
            if gsl + 2 < 64:
                load_wgu(gsl + 2)
            for th in range(2):
                bg = th
                bu = 2 + th
                ACT.wait(PE, tgm[th])
                if silu_free:
                    ACT.wait(DVE, silu_free)
                tsl_ = ACT.emit(lambda e, bg=bg, th=th: e.activation(
                    silu_sb[:, th * 512:(th + 1) * 512], psf[bg][:, :], AF.Silu))
                bankf[bg] = (ACT, tsl_)
                DVE.wait(ACT, tsl_)
                DVE.wait(PE, tum[th])
                if act_war:
                    DVE.wait(*act_war)
                    DVE.wait(*act_dve_war)
                tac = DVE.emit(lambda e, bu=bu, sl=sl, th=th: e.tensor_tensor(
                    act[:, sl * 1024 + th * 512: sl * 1024 + (th + 1) * 512],
                    psf[bu][:, :], silu_sb[:, th * 512:(th + 1) * 512], ALU.mult))
                bankf[bu] = (DVE, tac)
                silu_free = tac
                act_t[(sl, th)] = tac
            if r == 0 and sl == 0:
                tap_now("slu0", silu_sb)
        act_war = None
        act_ready = max(act_t.values())
        if r == 0:
            tap_now("act0", act)
        if r == 0:
            load_wd(0)
            load_wd(1)
        for dch in range(4):
            gdc = r * 4 + dch
            for tc in range(8):
                bd_ = 4 + tc % 2
                PE.wait(GPD, wd_tick[gdc])
                if bankf[bd_]:
                    PE.wait(*bankf[bd_])
                PE.wait(DVE, act_ready)
                tdm = None
                for hc in range(16):
                    fn = MMf(psf[bd_][:, :],
                             act[:, hc * 1024 + tc * 128: hc * 1024 + (tc + 1) * 128],
                             wd[gdc % 2][:, hc * 512:(hc + 1) * 512], hc == 0, hc == 15)
                    tdm = PE.emit(fn) if hc == 15 else PE.emit(fn, inc=False)
                DVE.wait(PE, tdm)
                DVE.wait(DVE, res_last[tc])
                tra = DVE.emit(lambda e, bd_=bd_, tc=tc, dch=dch: e.tensor_tensor(
                    res[:, tc * 2048 + dch * 512: tc * 2048 + (dch + 1) * 512],
                    psf[bd_][:, :],
                    res[:, tc * 2048 + dch * 512: tc * 2048 + (dch + 1) * 512],
                    ALU.add))
                bankf[bd_] = (DVE, tra)
                res_last[tc] = tra
            wd_last[gdc % 2] = PE.tick
            if gdc + 2 < 16:
                load_wd(gdc + 2)

    tstf = None
    for tc in range(8):
        SYD.wait(DVE, res_last[tc])
        tstf = SYD.emit(lambda e, tc=tc: e.dma_start(
            OUTd[tc * 128:(tc + 1) * 128, :], res[:, tc * 2048:(tc + 1) * 2048]))
    tstf = max(tstf, tap_store[0])
    SYD.wait(SYD, tstf)

    with nc.Block() as block:
        block.gpsimd(lambda g: GPD.replay(g))
        block.sync(lambda s: SYD.replay(s))
        block.tensor(lambda t: PE.replay(t))
        block.vector(lambda v: DVE.replay(v))
        block.scalar(lambda s: ACT.replay(s))
    return nc



def _pack_stream(qkv_w, gate_w, out_w, w_gate, w_up, w_down):
    idx = np.arange(DM)
    qrows = (idx // 64) * 192 + idx % 64
    WqT = np.ascontiguousarray(qkv_w[qrows].T.astype(np.float32))
    WkT = np.ascontiguousarray(qkv_w[qrows + 64].T.astype(np.float32))
    WvT = np.ascontiguousarray(qkv_w[qrows + 128].T.astype(np.float32))
    GWT = np.ascontiguousarray(gate_w.T.astype(np.float32))
    OWT = np.ascontiguousarray(out_w.T.astype(np.float32))
    WGT = np.ascontiguousarray(w_gate.T.astype(np.float32))
    WUT = np.ascontiguousarray(w_up.T.astype(np.float32))
    WDT = np.ascontiguousarray(w_down.T.astype(np.float32))
    WS = np.empty((128, STREAM_TOT), dtype=np.float32)
    for key, off in OFF.items():
        kind = key[0]
        if kind == "PK":
            _, q, ic = key
            WS[:, off:off + 512] = WkT[ic * 128:(ic + 1) * 128, q * 512:(q + 1) * 512]
        elif kind == "PV":
            _, q, ic = key
            WS[:, off:off + 512] = WvT[ic * 128:(ic + 1) * 128, q * 512:(q + 1) * 512]
        elif kind == "Q":
            _, q, cs, ic = key
            WS[:, off:off + 128] = WqT[ic * 128:(ic + 1) * 128,
                                       q * 512 + cs * 128: q * 512 + (cs + 1) * 128]
        elif kind == "K":
            _, q, cs, ic = key
            WS[:, off:off + 128] = WkT[ic * 128:(ic + 1) * 128,
                                       q * 512 + cs * 128: q * 512 + (cs + 1) * 128]
        elif kind == "V":
            _, q, ic = key
            WS[:, off:off + 512] = WvT[ic * 128:(ic + 1) * 128, q * 512:(q + 1) * 512]
        elif kind == "gw":
            _, gq, ic = key
            WS[:, off:off + 512] = GWT[ic * 128:(ic + 1) * 128, gq * 512:(gq + 1) * 512]
        elif kind == "ow":
            _, gq, dch, gch = key
            WS[:, off:off + 512] = OWT[gq * 512 + gch * 128: gq * 512 + (gch + 1) * 128,
                                       dch * 512:(dch + 1) * 512]
        elif kind == "wg":
            _, r, sl, ic = key
            WS[:, off:off + 128] = WGT[ic * 128:(ic + 1) * 128,
                                       (r * 16 + sl) * 128:(r * 16 + sl + 1) * 128]
        elif kind == "wu":
            _, r, sl, ic = key
            WS[:, off:off + 128] = WUT[ic * 128:(ic + 1) * 128,
                                       (r * 16 + sl) * 128:(r * 16 + sl + 1) * 128]
        elif kind == "wd":
            _, r, dch, hc = key
            WS[:, off:off + 512] = WDT[(r * 16 + hc) * 128:(r * 16 + hc + 1) * 128,
                                       dch * 512:(dch + 1) * 512]
        else:
            raise KeyError(kind)
    return WS


def _aux_arrays(norm_w):
    slt = np.array(slopes(), dtype=np.float64)
    p = np.arange(128)
    MND = np.full((128, 512), NEG, dtype=np.float32)
    for jc in range(2):
        j = jc * 128 + p[:, None]
        i = np.arange(256)[None, :]
        MND[:, jc * 256:(jc + 1) * 256] = np.where(i >= j, (j - i).astype(np.float64), NEG
                                                   ).astype(np.float32)
    KDP = np.empty((128, 4096), dtype=np.float32)
    for jh in range(2):
        jpos = jh * 128 + p
        for c in range(16):
            for fh in range(2):
                h = 2 * c + fh
                col = jh * 2048 + c * 128 + fh * 64
                KDP[:, col:col + 64] = np.exp(-slt[h] * (BS - (jpos + 1.0))
                                              )[:, None].astype(np.float32)
    QDB = np.empty((128, 4096), dtype=np.float32)
    jj = np.arange(256)[None, :] + 1.0
    for cg in range(16):
        hrow = 2 * cg + p // 64
        QDB[:, cg * 256:(cg + 1) * 256] = np.exp(-slt[hrow][:, None] * jj).astype(np.float32)
    NW = np.ascontiguousarray(norm_w.astype(np.float32).reshape(16, 128).T)
    IDT = np.eye(128, dtype=np.float32)
    return MND, KDP, QDB, NW, IDT


_nc_cache = {}


def kernel(L0_attn_hidden_states, L0_attn_qkv_w, L0_attn_out_w, L0_attn_gate_w,
           L0_attn_norm_w, L0_ffn_w_gate, L0_ffn_w_up, L0_ffn_w_down):
    x = np.ascontiguousarray(np.asarray(L0_attn_hidden_states, dtype=np.float32))
    WS = _pack_stream(np.asarray(L0_attn_qkv_w, dtype=np.float32),
                      np.asarray(L0_attn_gate_w, dtype=np.float32),
                      np.asarray(L0_attn_out_w, dtype=np.float32),
                      np.asarray(L0_ffn_w_gate, dtype=np.float32),
                      np.asarray(L0_ffn_w_up, dtype=np.float32),
                      np.asarray(L0_ffn_w_down, dtype=np.float32))
    MND, KDP, QDB, NW, _ = _aux_arrays(np.asarray(L0_attn_norm_w, dtype=np.float32))
    zeros = np.zeros((TK, DM), dtype=np.float32)
    in_maps = []
    for c in range(8):
        b, half = c // 2, c % 2
        xm = np.ascontiguousarray(x[b, half * TK:(half + 1) * TK, :])
        xp = zeros if half == 0 else np.ascontiguousarray(x[b, 0:TK, :])
        in_maps.append({"X": xm, "XP": xp, "WS": WS, "MND": MND, "KDP": KDP,
                        "QDB": QDB, "NW": NW})
    if "nc" not in _nc_cache:
        _nc_cache["nc"] = build_program()
    nc = _nc_cache["nc"]
    r = run_bass_kernel_spmd(nc, in_maps, core_ids=list(range(8)))
    out = np.empty((4, 2048, DM), dtype=np.float32)
    for c in range(8):
        b, half = c // 2, c % 2
        out[b, half * TK:(half + 1) * TK, :] = r.results[c]["OUT"]
    return out


# version 12
# speedup vs baseline: 1.0513x; 1.0513x over previous
import math
import numpy as np
import concourse.bass as bass
import concourse.mybir as mybir
from concourse.bass_utils import run_bass_kernel_spmd

F32 = mybir.dt.float32
BF16 = mybir.dt.bfloat16
AF = mybir.ActivationFunctionType
ALU = mybir.AluOpType
AX = mybir.AxisListType

H, D, BS = 32, 64, 256
DM = 2048
TK = 1024
FH = 8192
NEG = -1.0e30
GRP = 8192
EPS = 1e-5

_slopes = None


def slopes():
    global _slopes
    if _slopes is None:
        base = 1.0 / 2.0 ** (8.0 / H)
        fac = 1.0 - 0.0 / (H - 1 + 1e-5) + 1e-5
        _slopes = [float(base ** (h + 1) * fac) for h in range(H)]
    return _slopes



def make_plan():
    off = {}
    cur = 0

    def add(key, w):
        nonlocal cur
        off[key] = cur
        cur += w

    for q in range(4):
        for ic in range(16):
            add(("PK", q, ic), 512)
        for ic in range(16):
            add(("PV", q, ic), 512)
    for q in range(4):
        for cs in range(4):
            for ic in range(16):
                add(("Q", q, cs, ic), 128)
        for cs in range(4):
            for ic in range(16):
                add(("K", q, cs, ic), 128)
        for ic in range(16):
            add(("V", q, ic), 512)
    for gq in range(4):
        for ic in range(16):
            add(("gw", gq, ic), 512)
        for dch in range(4):
            for gch in range(4):
                add(("ow", gq, dch, gch), 512)
    for r in range(4):
        for sl in range(16):
            for ic in range(16):
                add(("wg", r, sl, ic), 128)
            for ic in range(16):
                add(("wu", r, sl, ic), 128)
        for dch in range(4):
            for hc in range(16):
                add(("wd", r, dch, hc), 512)
    return off, cur


OFF, STREAM_TOT = make_plan()
RING_GROUPS = []
for _q in range(4):
    RING_GROUPS.append(("PK", _q, 0))
    RING_GROUPS.append(("PV", _q, 0))
for _q in range(4):
    RING_GROUPS.append(("Q", _q, 0, 0))
    RING_GROUPS.append(("K", _q, 0, 0))
    RING_GROUPS.append(("V", _q, 0))
RING_START = {k: OFF[k] for k in RING_GROUPS}



class EngT:
    def __init__(self, sem, step=1):
        self.sem = sem
        self.step = step
        self.ops = []
        self.tick = 0
        self.waited = {}

    def wait(self, other, tick):
        if other is None or tick is None or tick <= 0:
            return
        if self.waited.get(id(other), 0) >= tick:
            return
        self.waited[id(other)] = tick
        self.ops.append(lambda e, o=other, t=tick: e.wait_ge(o.sem, t * o.step))

    def emit(self, fn, inc=True):
        if inc:
            self.tick += 1
            s, st = self.sem, self.step
            self.ops.append(lambda e, f=fn, s=s, st=st: f(e).then_inc(s, st))
            return self.tick
        self.ops.append(fn)
        return None

    def replay(self, e):
        for f in self.ops:
            f(e)



def build_program(taps=()):
    nc = bass.Bass("TRN2", target_bir_lowering=False)
    Xd = nc.dram_tensor("X", [TK, DM], F32, kind="ExternalInput").ap()
    XPd = nc.dram_tensor("XP", [TK, DM], F32, kind="ExternalInput").ap()
    WSd = nc.dram_tensor("WS", [128, STREAM_TOT], F32, kind="ExternalInput").ap()
    MNDd = nc.dram_tensor("MND", [128, 512], F32, kind="ExternalInput").ap()
    KDPd = nc.dram_tensor("KDP", [128, 4096], F32, kind="ExternalInput").ap()
    QDBd = nc.dram_tensor("QDB", [128, 4096], F32, kind="ExternalInput").ap()
    NWd = nc.dram_tensor("NW", [128, 16], F32, kind="ExternalInput").ap()
    OUTd = nc.dram_tensor("OUT", [TK, DM], F32, kind="ExternalOutput").ap()
    tap_d = {}
    for tname, tshape, tdt in [
        ("xn", [128, 16384], BF16), ("attn", [128, 16384], BF16),
        ("resb", [128, 16384], F32), ("S", [128, 2048], F32),
        ("kfm", [128, 4096], BF16), ("vtm", [128, 4096], BF16),
        ("kp", [128, 4096], BF16), ("vp", [128, 4096], BF16),
        ("xnp", [128, 16384], BF16), ("act0", [128, 16384], BF16),
        ("wgu0", [128, 4096], BF16), ("slu0", [128, 1024], BF16),
    ]:
        if tname in taps:
            tap_d[tname] = nc.dram_tensor("tap_" + tname, tshape, tdt,
                                          kind="ExternalOutput").ap()

    ARENA = 53080
    arena = nc.alloc_sbuf_tensor("arena", [128, ARENA], F32)
    base = nc.lookup_mloc(arena).addr

    def sb(name, shape, dtype, col):
        return nc.alloc_sbuf_tensor_at(name, shape, dtype, offset=base + col * 4).ap()

    xn = sb("xn", [128, 16384], BF16, 0)
    res = sb("res", [128, 16384], F32, 8192)
    xnp = sb("xnp", [128, 16384], BF16, 8192)
    mKx = [sb("mK0", [128, 4096], BF16, 8192), sb("mK1", [128, 4096], BF16, 16384)]
    mVx = [sb("mV0", [128, 4096], BF16, 10240), sb("mV1", [128, 4096], BF16, 18432)]
    mQx = [sb("mQ0", [128, 4096], BF16, 12288), sb("mQ1", [128, 4096], BF16, 20480)]
    mKfx = [sb("mKf0", [128, 4096], BF16, 14336), sb("mKf1", [128, 4096], BF16, 22528)]
    xt = [sb("xt0", [128, 2048], F32, 16384), sb("xt1", [128, 2048], F32, 18432)]
    sq = sb("sq", [128, 2048], F32, 20480)
    xnt = sb("xnt", [128, 2048], BF16, 22528)
    s1v = sb("s1v", [128, 1], F32, 53016)
    s2v = sb("s2v", [128, 1], F32, 53024)
    mv = sb("mv", [128, 1], F32, 53032)
    m2v = sb("m2v", [128, 1], F32, 53040)
    v1v = sb("v1v", [128, 1], F32, 53048)
    v2v = sb("v2v", [128, 1], F32, 53056)
    sdv = sb("sdv", [128, 1], F32, 53064)
    attn = sb("attn", [128, 16384], BF16, 24576)
    act = attn
    S = sb("S", [128, 2048], F32, 32768)
    wr = [sb("wr0", [128, GRP], BF16, 34816), sb("wr1", [128, GRP], BF16, 38912)]
    kp = sb("kp", [128, 4096], BF16, 43008)
    vp = sb("vp", [128, 4096], BF16, 45056)
    ptmp = sb("ptmp", [128, 512], BF16, 47104)
    qsc = sb("qsc", [128, 1024], F32, 47104)
    ddT = sb("ddT", [128, 4096], BF16, 43008)
    awb = [sb("awb0", [128, 512], BF16, 48136),
           sb("awb1", [128, 512], BF16, 48392)]
    stg2 = sb("stg2", [128, 2048], F32, 43008)
    gwx = [sb("gw0", [128, 8192], BF16, 36864),
           sb("gw1", [128, 8192], BF16, 32768)]
    owc = [sb("owc0", [128, 2048], BF16, 40960),
           sb("owc1", [128, 2048], BF16, 43008)]
    gfm_all = sb("gfm_all", [128, 4096], BF16, 45056)
    sqB = sb("sqB", [128, 2048], F32, 45056)
    gate_sbx = [sb("gate_sb0", [128, 512], BF16, 47104),
                sb("gate_sb1", [128, 512], BF16, 47360)]
    gtmx = [sb("gtm0", [128, 512], BF16, 47616), sb("gtm1", [128, 512], BF16, 47872)]
    rstdB = sb("rstdB", [128, 8], F32, 48128)
    wgu = [sb("wgu0", [128, 4096], BF16, 32768), sb("wgu1", [128, 4096], BF16, 34816)]
    wd = [sb("wd0", [128, 8192], BF16, 36864), sb("wd1", [128, 8192], BF16, 40960)]
    silu_sb = sb("silu_sb", [128, 1024], BF16, 45056)
    MND = sb("MND", [128, 512], BF16, 48648)
    KDP = sb("KDP", [128, 4096], BF16, 48904)
    QDB = sb("QDB", [128, 4096], BF16, 50952)
    NW = sb("NW", [128, 16], F32, 53000)

    psf = [nc.alloc_psum_tensor("psf%d" % j, [128, 512], F32).ap() for j in range(8)]

    sem_gp = nc.alloc_semaphore("s_gp")
    sem_sy = nc.alloc_semaphore("s_sy")
    sem_pe = nc.alloc_semaphore("s_pe")
    sem_dv = nc.alloc_semaphore("s_dv")
    sem_ac = nc.alloc_semaphore("s_ac")
    GPD = EngT(sem_gp, 16)
    SYD = EngT(sem_sy, 16)
    PE = EngT(sem_pe, 1)
    DVE = EngT(sem_dv, 1)
    ACT = EngT(sem_ac, 1)

    GPD.emit(lambda e: e.dma_start(MND[:, :], MNDd))
    GPD.emit(lambda e: e.dma_start(KDP[:, :], KDPd))
    GPD.emit(lambda e: e.dma_start(QDB[:, :], QDBd))
    t_aux = GPD.emit(lambda e: e.dma_start(NW[:, :], NWd))

    ring_last = {0: None, 1: None}
    ring_tick = {}
    n_loaded = [0]

    def ring_load_next():
        g = n_loaded[0]
        if g >= len(RING_GROUPS):
            return
        n_loaded[0] = g + 1
        buf = g % 2
        c0 = RING_START[RING_GROUPS[g]]
        if ring_last[buf] is not None:
            GPD.wait(*ring_last[buf])
        ring_tick[g] = GPD.emit(
            lambda e, buf=buf, c0=c0: e.dma_start(wr[buf][:, :], WSd[:, c0:c0 + GRP]))

    def ring_tile(gidx, key, width):
        buf = gidx % 2
        rel = OFF[key] - RING_START[RING_GROUPS[gidx]]
        assert 0 <= rel and rel + width <= GRP
        return wr[buf][:, rel:rel + width]

    ring_load_next()
    ring_load_next()

    def MMf(out_ap, l_ap, r_ap, start, stop, tp=None):
        return lambda e: e.matmul(out_ap, l_ap, r_ap, start=start, stop=stop,
                                  tile_position=tp)

    ticks = {}
    tap_store = [0]

    def tap_now(name, src, stage=None):
        if name not in tap_d:
            return
        dap = tap_d[name]
        SYD.wait(DVE, DVE.tick)
        SYD.wait(ACT, ACT.tick)
        SYD.wait(PE, PE.tick)
        SYD.wait(GPD, GPD.tick)
        tap_store[0] = SYD.emit(lambda e, dap=dap, src=src: e.dma_start(dap, src[:, :]))
        DVE.wait(SYD, tap_store[0])
        ACT.wait(SYD, tap_store[0])
        GPD.wait(SYD, tap_store[0])
        PE.wait(SYD, tap_store[0])

    xt_free = {0: (None, None), 1: (None, None)}
    ln_state = {"iter": 0, "sq_free": None, "s_free": None, "xnt_rd": 0}
    psb_free = {0: None, 1: None}

    def ln_pre(part, tc):
        src = Xd if part == 0 else XPd
        b = ln_state["iter"] % 2
        ln_state["iter"] += 1
        dvf, acf = xt_free[b]
        SYD.wait(DVE, dvf)
        SYD.wait(ACT, acf)
        tld = SYD.emit(lambda e, b=b, src=src, tc=tc: e.dma_start(
            xt[b][:, :], src[tc * 128:(tc + 1) * 128, :]))
        ACT.wait(SYD, tld)
        if ln_state["s_free"]:
            ACT.wait(DVE, ln_state["s_free"])
        if ln_state["sq_free"] is not None:
            ACT.wait(DVE, ln_state["sq_free"])
        t_s1 = ACT.emit(lambda e, b=b: e.activation(sq[:, :], xt[b][:, :], AF.Copy,
                                                    accum_out=s1v[:, :]))
        t_sq = ACT.emit(lambda e, b=b: e.activation(sq[:, :], xt[b][:, :], AF.Square,
                                                    accum_out=s2v[:, :]))
        DVE.wait(ACT, t_sq)
        t3 = DVE.emit(lambda e: e.tensor_scalar(mv[:, :], s1v[:, :], 1.0 / DM, None, ALU.mult))
        t4 = DVE.emit(lambda e: e.tensor_scalar(v1v[:, :], s2v[:, :], 1.0 / DM, EPS, ALU.mult, ALU.add))
        ln_state["s_free"] = t4
        ln_state["sq_free"] = t4
        DVE.wait(DVE, t3)
        DVE.wait(DVE, t4)
        DVE.emit(lambda e: e.tensor_tensor(m2v[:, :], mv[:, :], mv[:, :], ALU.mult))
        DVE.wait(DVE, DVE.tick)
        t6 = DVE.emit(lambda e: e.tensor_tensor(v2v[:, :], v1v[:, :], m2v[:, :], ALU.subtract))
        ACT.wait(DVE, t6)
        t7 = ACT.emit(lambda e: e.activation(sdv[:, :], v2v[:, :], AF.Sqrt))
        DVE.wait(ACT, t7)
        t8 = DVE.emit(lambda e: e.reciprocal(s1v[:, :], sdv[:, :]))
        DVE.wait(DVE, t8)
        DVE.wait(SYD, ln_state["xnt_rd"])
        t9 = DVE.emit(lambda e, b=b: e.tensor_scalar(
            xnt[:, :], xt[b][:, :], mv[:, :], s1v[:, :], ALU.subtract, ALU.mult))
        xt_free[b] = (t9, t_sq)
        return (part, tc, t9)

    def ln_post(st):
        part, tc, t9 = st
        dst = xn if part == 0 else xnp
        SYD.wait(DVE, t9)
        d3 = dst.rearrange("p (ic t) -> p ic t", t=1024)[:, :, tc * 128:(tc + 1) * 128]
        s3 = xnt.rearrange("p (ic t) -> p ic t", t=128)
        ttr = SYD.emit(lambda e, d3=d3, s3=s3: e.dma_start(d3, s3, transpose=True))
        ln_state["xnt_rd"] = ttr
        return ttr

    xnp_tick = {}
    _lnq = None
    for _tc in range(8):
        _st = ln_pre(1, _tc)
        if _lnq is not None:
            xnp_tick[_lnq[1]] = ln_post(_lnq)
        _lnq = _st
    xnp_tick[_lnq[1]] = ln_post(_lnq)
    ticks["xnp"] = max(xnp_tick.values())
    ticks["xn"] = None

    t_ms = DVE.emit(lambda e: e.memset(S[:, :], 0.0))
    ticks["S0"] = t_ms
    bankf = {j: None for j in range(8)}
    sltK = slopes()
    bd = [math.exp(-sv * BS) for sv in sltK]
    kp_t = {}
    vp_t = {}
    ptmp_free = None
    stt_prev = {}
    sbf_t = {}
    gidx = [0]
    q_t = [{}, {}]
    kf_t = [{}, {}]
    ktm_t = [{}, {}]
    v_t = [{}, {}]
    ddT_t = {}
    mask_last = [0]
    po_free = {0: None, 1: None}
    aw_free = {}
    awb_rd = {}
    qsc_t_ref = [0]
    scan_state = {"pe": 0, "dve": 0}
    attn_ticks = []

    def proj_gen(q):
        si = q % 2
        mQ, mKf, mV, mK = mQx[si], mKfx[si], mVx[si], mKx[si]
        q_t[si].clear(); kf_t[si].clear(); ktm_t[si].clear(); v_t[si].clear()
        war = (DVE, scan_state["dve"]) if scan_state["dve"] else None
        for cs in range(4):
            for th in range(2):
                bank = (cs * 2 + th) % 2
                g = gidx[0]
                PE.wait(GPD, ring_tick[g])
                if bankf[bank]:
                    PE.wait(*bankf[bank])
                PE.wait(SYD, ticks["xn"])
                tmm = None
                for ic in range(16):
                    fn = MMf(psf[bank][:, :], ring_tile(g, ("Q", q, cs, ic), 128),
                             xn[:, ic * 1024 + th * 512: ic * 1024 + (th + 1) * 512],
                             ic == 0, ic == 15)
                    tmm = PE.emit(fn) if ic == 15 else PE.emit(fn, inc=False)
                ring_last[g % 2] = (PE, tmm)
                ACT.wait(PE, tmm)
                if war:
                    ACT.wait(*war)
                tq = ACT.emit(lambda e, bank=bank, cs=cs, th=th, mQ=mQ: e.activation(
                    mQ[:, cs * 1024 + th * 512: cs * 1024 + (th + 1) * 512],
                    psf[bank][:, :], AF.Silu))
                bankf[bank] = (ACT, tq)
                q_t[si][(cs, th)] = tq
                yield
        ring_load_next()
        gidx[0] += 1
        for cs in range(4):
            for th in range(2):
                bank = (cs * 2 + th) % 2
                g = gidx[0]
                PE.wait(GPD, ring_tick[g])
                if bankf[bank]:
                    PE.wait(*bankf[bank])
                tmm = None
                for ic in range(16):
                    fn = MMf(psf[bank][:, :], ring_tile(g, ("K", q, cs, ic), 128),
                             xn[:, ic * 1024 + th * 512: ic * 1024 + (th + 1) * 512],
                             ic == 0, ic == 15)
                    tmm = PE.emit(fn) if ic == 15 else PE.emit(fn, inc=False)
                ring_last[g % 2] = (PE, tmm)
                ACT.wait(PE, tmm)
                if war:
                    ACT.wait(*war)
                tk = ACT.emit(lambda e, bank=bank, cs=cs, th=th, mKf=mKf: e.activation(
                    mKf[:, cs * 1024 + th * 512: cs * 1024 + (th + 1) * 512],
                    psf[bank][:, :], AF.Silu))
                bankf[bank] = (ACT, tk)
                kf_t[si][(cs, th)] = tk
                yield
        ring_load_next()
        gidx[0] += 1
        for tc in range(8):
            bank = tc % 2
            g = gidx[0]
            PE.wait(GPD, ring_tick[g])
            if bankf[bank]:
                PE.wait(*bankf[bank])
            tmm = None
            for ic in range(16):
                fn = MMf(psf[bank][:, :],
                         xn[:, ic * 1024 + tc * 128: ic * 1024 + (tc + 1) * 128],
                         ring_tile(g, ("V", q, ic), 512), ic == 0, ic == 15)
                tmm = PE.emit(fn) if ic == 15 else PE.emit(fn, inc=False)
            ring_last[g % 2] = (PE, tmm)
            ACT.wait(PE, tmm)
            if war:
                ACT.wait(*war)
            tv = ACT.emit(lambda e, bank=bank, tc=tc, mV=mV: e.activation(
                mV[:, tc * 512:(tc + 1) * 512], psf[bank][:, :], AF.Silu))
            bankf[bank] = (ACT, tv)
            v_t[si][tc] = tv
            yield
        ring_load_next()
        gidx[0] += 1
        ttrs = None
        for cs in range(4):
            SYD.wait(ACT, kf_t[si][(cs, 0)])
            SYD.wait(ACT, kf_t[si][(cs, 1)])
            if war:
                SYD.wait(*war)
            s3 = mKf[:, cs * 1024:(cs + 1) * 1024].rearrange("p (tc t) -> p tc t", t=128)
            d3 = mK.rearrange("p (tc c) -> p tc c", c=512)[:, :, cs * 128:(cs + 1) * 128]
            ttrs = SYD.emit(lambda e, s3=s3, d3=d3: e.dma_start(d3, s3, transpose=True))
        for tc in range(8):
            DVE.wait(SYD, ttrs)
            if war:
                DVE.wait(*war)
            tev = DVE.emit(lambda e, tc=tc, q=q, mK=mK: e.tensor_tensor(
                mK[:, tc * 512:(tc + 1) * 512],
                mK[:, tc * 512:(tc + 1) * 512],
                KDP[:, (tc % 2) * 2048 + q * 512:(tc % 2) * 2048 + (q + 1) * 512],
                ALU.mult))
            for cs in range(4):
                ktm_t[si][(cs, tc)] = tev
            if tc % 2 == 1:
                yield

    def prefix_scan(qp, gen):
        for lh in range(8):
            h = 8 * qp + lh
            rb = (lh % 2) * 64
            for blk in range(4):
                if bankf[5]:
                    PE.wait(*bankf[5])
                tC = None
                for jc in range(2):
                    tcj = blk * 2 + jc
                    PE.wait(DVE, kp_t[tcj])
                    PE.wait(ACT, vp_t[tcj])
                    fn = MMf(psf[5][rb:rb + 64, 0:64],
                             kp[:, tcj * 512 + lh * 64: tcj * 512 + (lh + 1) * 64],
                             vp[:, tcj * 512 + lh * 64: tcj * 512 + (lh + 1) * 64],
                             jc == 0, jc == 1, tp=(0, rb))
                    tC = PE.emit(fn) if jc == 1 else PE.emit(fn, inc=False)
                DVE.wait(PE, tC)
                DVE.wait(DVE, stt_prev.get(h))
                DVE.wait(DVE, ticks["S0"])
                tst = DVE.emit(lambda e, h=h, rb=rb, bdh=bd[h]: e.scalar_tensor_tensor(
                    S[rb:rb + 64, h * 64:(h + 1) * 64], S[rb:rb + 64, h * 64:(h + 1) * 64],
                    bdh, psf[5][rb:rb + 64, 0:64], ALU.mult, ALU.add))
                stt_prev[h] = tst
                bankf[5] = (DVE, tst)
                if gen is not None:
                    next(gen, None)

    def main_scan(q, gen):
        si = q % 2
        mQ, mKf, mV, mK = mQx[si], mKfx[si], mVx[si], mKx[si]
        for lh in range(8):
            ACT.wait(GPD, t_aux)
            ACT.wait(PE, p1_pe_done)
            ACT.wait(DVE, p1_dve_done)
            ACT.wait(DVE, mask_last[0])
            ddT_t[lh] = ACT.emit(lambda e, lh=lh, sv=sltK[8 * q + lh]: e.activation(
                ddT[:, lh * 512:(lh + 1) * 512], MND[:, :], AF.Exp, scale=sv))
        PE.wait(ACT, v_t[si][7])
        for _bk in (2, 3, 4, 5, 6, 7):
            if bankf[_bk]:
                PE.wait(*bankf[_bk])
                bankf[_bk] = None
        units = [(blk, lh) for blk in range(4) for lh in range(8)]

        def emit_aw(i):
            blk, lh = units[i]
            cs = lh // 2
            rb = (lh % 2) * 64
            ab = i % 2
            awbank = psf[2] if ab == 0 else psf[4]
            if aw_free.get(ab):
                PE.wait(DVE, aw_free[ab])
            PE.wait(ACT, ddT_t[lh])
            tA = None
            for jc in range(2):
                tcj = blk * 2 + jc
                fn = MMf(awbank[:, jc * 256:(jc + 1) * 256],
                         mKf[rb:rb + 64, cs * 1024 + tcj * 128: cs * 1024 + (tcj + 1) * 128],
                         mQ[rb:rb + 64, cs * 1024 + blk * 256: cs * 1024 + (blk + 1) * 256],
                         True, True)
                tA = PE.emit(fn) if jc == 1 else PE.emit(fn, inc=False)
            DVE.wait(PE, tA)
            DVE.wait(PE, awb_rd.get(ab, 0))
            tM = DVE.emit(lambda e, lh=lh, ab=ab, awbank=awbank: e.tensor_tensor(
                awb[ab][:, :], awbank[:, :], ddT[:, lh * 512:(lh + 1) * 512], ALU.mult))
            aw_free[ab] = tM
            mask_last[0] = tM
            return tM

        def emit_qsc(blk, t_prev_rd):
            DVE.wait(PE, t_prev_rd)
            for cs in range(4):
                DVE.wait(ACT, q_t[si][(cs, 0)])
                DVE.wait(ACT, q_t[si][(cs, 1)])
                qsc_t_ref[0] = DVE.emit(lambda e, cs=cs, blk=blk, q=q, mQ=mQ: e.tensor_tensor(
                    qsc[:, cs * 256:(cs + 1) * 256],
                    mQ[:, cs * 1024 + blk * 256: cs * 1024 + (blk + 1) * 256],
                    QDB[:, (q * 4 + cs) * 256:(q * 4 + cs + 1) * 256], ALU.mult))

        emit_qsc(0, 0)
        tmask = {0: emit_aw(0)}
        tO_prev = 0
        for i in range(32):
            blk, lh = units[i]
            cs = lh // 2
            rb = (lh % 2) * 64
            h = 8 * q + lh
            ab = i % 2
            if i % 8 == 0 and i > 0:
                emit_qsc(blk, tO_prev)
            if i + 1 < 32:
                tmask[i + 1] = emit_aw(i + 1)
            if gen is not None:
                next(gen, None)
            PE.wait(DVE, tmask[i])
            PE.wait(DVE, qsc_t_ref[0])
            if i % 8 == 0:
                for ich in range(2):
                    if po_free[ich]:
                        PE.wait(ACT, po_free[ich])
            PE.wait(DVE, stt_prev.get(h))
            tO = None
            for ich in range(2):
                po = psf[3] if ich == 0 else psf[7]
                scol = lh * 64
                PE.emit(MMf(po[:, scol:scol + 64],
                            awb[ab][:, ich * 128:(ich + 1) * 128],
                            mV[:, (blk * 2) * 512 + lh * 64: (blk * 2) * 512 + (lh + 1) * 64],
                            True, False), inc=False)
                PE.emit(MMf(po[:, scol:scol + 64],
                            awb[ab][:, 256 + ich * 128: 256 + (ich + 1) * 128],
                            mV[:, (blk * 2 + 1) * 512 + lh * 64: (blk * 2 + 1) * 512 + (lh + 1) * 64],
                            False, False), inc=False)
                fn = MMf(po[:, scol:scol + 64],
                         qsc[rb:rb + 64, cs * 256 + ich * 128: cs * 256 + (ich + 1) * 128],
                         S[rb:rb + 64, h * 64:(h + 1) * 64], False, True)
                tO = PE.emit(fn)
            awb_rd[ab] = tO
            tO_prev = tO
            cb = 5 if i % 2 == 0 else 6
            if bankf[cb]:
                PE.wait(*bankf[cb])
            tC = None
            for jc in range(2):
                tcj = blk * 2 + jc
                PE.wait(DVE, ktm_t[si][(cs, tcj)])
                PE.wait(ACT, v_t[si][tcj])
                fn = MMf(psf[cb][rb:rb + 64, 0:64],
                         mK[:, tcj * 512 + lh * 64: tcj * 512 + (lh + 1) * 64],
                         mV[:, tcj * 512 + lh * 64: tcj * 512 + (lh + 1) * 64],
                         jc == 0, jc == 1, tp=(0, rb))
                tC = PE.emit(fn) if jc == 1 else PE.emit(fn, inc=False)
            DVE.wait(PE, tC)
            DVE.wait(DVE, stt_prev.get(h))
            tst = DVE.emit(lambda e, h=h, rb=rb, cb=cb, bdh=bd[h]: e.scalar_tensor_tensor(
                S[rb:rb + 64, h * 64:(h + 1) * 64], S[rb:rb + 64, h * 64:(h + 1) * 64],
                bdh, psf[cb][rb:rb + 64, 0:64], ALU.mult, ALU.add))
            stt_prev[h] = tst
            bankf[cb] = (DVE, tst)
            if i % 8 == 7:
                for ich in range(2):
                    po = psf[3] if ich == 0 else psf[7]
                    ACT.wait(PE, tO)
                    tev = ACT.emit(lambda e, po=po, blk=blk, ich=ich, q=q: e.activation(
                        attn[:, (blk * 2 + ich) * 2048 + q * 512:
                             (blk * 2 + ich) * 2048 + (q + 1) * 512],
                        po[:, :], AF.Copy))
                    po_free[ich] = tev
                    attn_ticks.append(tev)


    for qp in range(4):
        for kind in ("PK", "PV"):
            ln_main = qp * 2 + (0 if kind == "PK" else 1)
            _mst = ln_pre(0, ln_main)
            for tc in range(8):
                bank = (qp * 8 + tc) % 2
                g = gidx[0]
                PE.wait(GPD, ring_tick[g])
                if bankf[bank]:
                    PE.wait(*bankf[bank])
                PE.wait(SYD, ticks["xnp"])
                tmm = None
                for ic in range(16):
                    fn = MMf(psf[bank][:, :],
                             xnp[:, ic * 1024 + tc * 128: ic * 1024 + (tc + 1) * 128],
                             ring_tile(g, (kind, qp, ic), 512), ic == 0, ic == 15)
                    tmm = PE.emit(fn) if ic == 15 else PE.emit(fn, inc=False)
                ring_last[g % 2] = (PE, tmm)
                ACT.wait(PE, tmm)
                if kind == "PK":
                    if ptmp_free:
                        ACT.wait(DVE, ptmp_free)
                    tsl = ACT.emit(lambda e, bank=bank: e.activation(
                        ptmp[:, :], psf[bank][:, :], AF.Silu))
                    bankf[bank] = (ACT, tsl)
                    DVE.wait(ACT, tsl)
                    tkp = DVE.emit(lambda e, tc=tc, qp=qp: e.tensor_tensor(
                        kp[:, tc * 512:(tc + 1) * 512], ptmp[:, :],
                        KDP[:, (tc % 2) * 2048 + qp * 512:(tc % 2) * 2048 + (qp + 1) * 512],
                        ALU.mult))
                    ptmp_free = tkp
                    kp_t[tc] = tkp
                else:
                    tsl = ACT.emit(lambda e, bank=bank, tc=tc: e.activation(
                        vp[:, tc * 512:(tc + 1) * 512], psf[bank][:, :], AF.Silu))
                    bankf[bank] = (ACT, tsl)
                    vp_t[tc] = tsl
            ring_load_next()
            gidx[0] += 1
            _xn_syd = ln_post(_mst)
            if ln_main == 7:
                ticks["xn"] = _xn_syd
        if qp == 3:
            gen0 = proj_gen(0)
            prefix_scan(qp, gen0)
            for _ in gen0:
                pass
        else:
            prefix_scan(qp, None)

    tap_now("S", S, sq)
    tap_now("kp", kp, sq)
    tap_now("vp", vp, sq)
    tap_now("xn", xn, sq)
    p1_pe_done = PE.tick
    p1_dve_done = DVE.tick

    for q in range(4):
        gen_next = proj_gen(q + 1) if q < 3 else None
        main_scan(q, gen_next)
        if gen_next is not None:
            for _ in gen_next:
                pass
        if aw_free.get(0):
            bankf[2] = (DVE, aw_free[0])
        if aw_free.get(1):
            bankf[4] = (DVE, aw_free[1])
        if po_free[0]:
            bankf[3] = (ACT, po_free[0])
        if po_free[1]:
            bankf[7] = (ACT, po_free[1])
        scan_state["pe"] = PE.tick
        scan_state["dve"] = DVE.tick
    scan_pe_done = PE.tick
    scan_dve_done = DVE.tick
    ticks["attn"] = max(attn_ticks)
    tap_now("attn", attn, stg2)
    tap_now("kfm", mKfx[1], stg2)
    tap_now("vtm", mVx[1], stg2)
    scan_dve_done = DVE.tick

    for _j in (2, 4):
        bankf[_j] = (DVE, scan_dve_done)
    sqB_free = None
    rstd_t = {}
    ACT.wait(DVE, scan_dve_done)
    ACT.wait(PE, scan_pe_done)
    for tc in range(8):
        ACT.wait(ACT, ticks["attn"])
        if sqB_free:
            ACT.wait(DVE, sqB_free)
        tsq = ACT.emit(lambda e, tc=tc: e.activation(
            sqB[:, :], attn[:, tc * 2048:(tc + 1) * 2048], AF.Square))
        DVE.wait(ACT, tsq)
        tr = DVE.emit(lambda e: e.tensor_reduce(s2v[:, :], sqB[:, :], AX.X, ALU.add))
        sqB_free = tr
        DVE.wait(DVE, tr)
        tv_ = DVE.emit(lambda e: e.tensor_scalar(
            v1v[:, :], s2v[:, :], 1.0 / DM, EPS, ALU.mult, ALU.add))
        ACT.wait(DVE, tv_)
        ts_ = ACT.emit(lambda e: e.activation(sdv[:, :], v1v[:, :], AF.Sqrt))
        DVE.wait(ACT, ts_)
        rstd_t[tc] = DVE.emit(lambda e, tc=tc: e.reciprocal(rstdB[:, tc:tc + 1], sdv[:, :]))
    xload = {}
    SYD.wait(SYD, tap_store[0])
    SYD.wait(PE, scan_pe_done)
    SYD.wait(DVE, scan_dve_done)
    for tc in range(8):
        xload[tc] = SYD.emit(lambda e, tc=tc: e.dma_start(
            res[:, tc * 2048:(tc + 1) * 2048], Xd[tc * 128:(tc + 1) * 128, :]))
    res_last = {}
    gw_war = {0: None, 1: None}
    ow_war = {0: None, 1: None}
    ow_tick = {}
    gate_free = {0: None, 1: None}
    gtm_rd = {0: None, 1: None}
    gfm_wr = 0
    gfm_rd = 0
    gw_tick = {}

    def load_gw(gq):
        b = gq % 2
        if gw_war[b]:
            GPD.wait(PE, gw_war[b])
        elif b == 1:
            GPD.wait(PE, scan_pe_done)
            GPD.wait(DVE, scan_dve_done)
        else:
            GPD.wait(PE, ring_last[0][1] if ring_last[0] else 0)
            GPD.wait(PE, ring_last[1][1] if ring_last[1] else 0)
        gw_tick[gq] = GPD.emit(lambda e, gq=gq, b=b: e.dma_start(
            gwx[b][:, :], WSd[:, OFF[("gw", gq, 0)]: OFF[("gw", gq, 0)] + GRP]))

    def load_ow(gq, dch):
        i = gq * 4 + dch
        b = i % 2
        if ow_war[b]:
            GPD.wait(PE, ow_war[b])
        else:
            GPD.wait(DVE, scan_dve_done)
        c0 = OFF[("ow", gq, dch, 0)]
        ow_tick[i] = GPD.emit(lambda e, c0=c0, b=b: e.dma_start(
            owc[b][:, :], WSd[:, c0:c0 + 2048]))

    load_gw(0)
    for gq in range(4):
        if gq + 1 < 4:
            load_gw(gq + 1)
        tg_arr = {}

        def gate_mms(tc, gq=gq):
            bank = tc % 2
            PE.wait(GPD, gw_tick[gq])
            if bankf[bank]:
                PE.wait(*bankf[bank])
            PE.wait(SYD, ticks["xn"])
            tg = None
            for ic in range(16):
                fn = MMf(psf[bank][:, :],
                         xn[:, ic * 1024 + tc * 128: ic * 1024 + (tc + 1) * 128],
                         gwx[gq % 2][:, ic * 512:(ic + 1) * 512], ic == 0, ic == 15)
                tg = PE.emit(fn) if ic == 15 else PE.emit(fn, inc=False)
            tg_arr[tc] = tg

        gate_mms(0)
        load_ow(gq, 0)
        load_ow(gq, 1)
        for tc in range(8):
            if tc + 1 < 8:
                gate_mms(tc + 1)
            gb = tc % 2
            ACT.wait(PE, tg_arr[tc])
            ACT.wait(DVE, gate_free[gb])
            tsg = ACT.emit(lambda e, gb=gb, bank=tc % 2: e.activation(
                gate_sbx[gb][:, :], psf[bank][:, :], AF.Sigmoid))
            bankf[tc % 2] = (ACT, tsg)
            DVE.wait(ACT, tsg)
            DVE.wait(DVE, rstd_t[tc])
            DVE.wait(SYD, gtm_rd[gb])
            tgt = DVE.emit(lambda e, tc=tc, gq=gq, gb=gb: e.scalar_tensor_tensor(
                gtmx[gb][:, :], attn[:, tc * 2048 + gq * 512: tc * 2048 + (gq + 1) * 512],
                rstdB[:, tc:tc + 1], gate_sbx[gb][:, :], ALU.mult, ALU.mult))
            gate_free[gb] = tgt
            SYD.wait(DVE, tgt)
            SYD.wait(DVE, gfm_wr)
            SYD.wait(PE, gfm_rd)
            s3 = gtmx[gb].rearrange("p (g t) -> p g t", t=128)
            d3 = gfm_all[:, tc * 512:(tc + 1) * 512].rearrange("p (g t) -> p g t", t=128)
            ttp = SYD.emit(lambda e, s3=s3, d3=d3: e.dma_start(d3, s3, transpose=True))
            gtm_rd[gb] = ttp
            tfm = None
            for gch in range(4):
                DVE.wait(SYD, ttp)
                tfm = DVE.emit(lambda e, gch=gch, gq=gq, tc=tc: e.tensor_scalar(
                    gfm_all[:, tc * 512 + gch * 128: tc * 512 + (gch + 1) * 128],
                    gfm_all[:, tc * 512 + gch * 128: tc * 512 + (gch + 1) * 128],
                    NW[:, gq * 4 + gch: gq * 4 + gch + 1], None, ALU.mult))
            gfm_wr = tfm
        tpo_last = None
        for dch in range(4):
            b = (gq * 4 + dch) % 2
            for tc in range(8):
                bo = 2 + tc % 2
                if bankf[bo]:
                    PE.wait(*bankf[bo])
                PE.wait(DVE, gfm_wr)
                PE.wait(GPD, ow_tick[gq * 4 + dch])
                tpo = None
                for gch in range(4):
                    fn = MMf(psf[bo][:, :],
                             gfm_all[:, tc * 512 + gch * 128: tc * 512 + (gch + 1) * 128],
                             owc[b][:, gch * 512:(gch + 1) * 512],
                             gch == 0, gch == 3)
                    tpo = PE.emit(fn) if gch == 3 else PE.emit(fn, inc=False)
                DVE.wait(PE, tpo)
                DVE.wait(SYD, xload[tc])
                tra = DVE.emit(lambda e, bo=bo, tc=tc, dch=dch: e.tensor_tensor(
                    res[:, tc * 2048 + dch * 512: tc * 2048 + (dch + 1) * 512],
                    psf[bo][:, :],
                    res[:, tc * 2048 + dch * 512: tc * 2048 + (dch + 1) * 512],
                    ALU.add))
                bankf[bo] = (DVE, tra)
                res_last[tc] = tra
                tpo_last = tpo
            ow_war[b] = tpo_last
            if dch + 2 < 4:
                load_ow(gq, dch + 2)
        gw_war[gq % 2] = tg_arr[7]
        gfm_rd = PE.tick
    gwow_war = PE.tick

    tap_now("resb", res, stg2)

    p3_dve_done = DVE.tick
    p3_act_done = ACT.tick
    wgu_last = {0: None, 1: None}
    wd_last = {0: None, 1: None}
    wgu_tick = {}
    wd_tick = {}
    act_war = (ACT, p3_act_done)
    act_dve_war = (DVE, p3_dve_done)

    def load_wgu(gsl):
        r, sl = gsl // 16, gsl % 16
        b = gsl % 2
        if wgu_last[b]:
            GPD.wait(PE, wgu_last[b])
        else:
            GPD.wait(PE, gwow_war)
        wgu_tick[gsl] = GPD.emit(lambda e, r=r, sl=sl, b=b: e.dma_start(
            wgu[b][:, :], WSd[:, OFF[("wg", r, sl, 0)]: OFF[("wg", r, sl, 0)] + 4096]))

    def load_wd(gdc):
        r, dch = gdc // 4, gdc % 4
        b = gdc % 2
        if wd_last[b]:
            GPD.wait(PE, wd_last[b])
        else:
            GPD.wait(PE, gwow_war)
            GPD.wait(DVE, p3_dve_done)
            GPD.wait(ACT, p3_act_done)
        wd_tick[gdc] = GPD.emit(lambda e, r=r, dch=dch, b=b: e.dma_start(
            wd[b][:, :], WSd[:, OFF[("wd", r, dch, 0)]: OFF[("wd", r, dch, 0)] + GRP]))

    load_wgu(0)
    load_wgu(1)
    tap_now("wgu0", wgu[0])
    act_t = {}
    silu_free = None
    for r in range(4):
        for sl in range(16):
            gsl = r * 16 + sl
            tum = {}
            tgm = {}
            for th in range(2):
                bg = th
                bu = 2 + th
                PE.wait(GPD, wgu_tick[gsl])
                if bankf[bg]:
                    PE.wait(*bankf[bg])
                t_ = None
                for ic in range(16):
                    fn = MMf(psf[bg][:, :], wgu[gsl % 2][:, ic * 128:(ic + 1) * 128],
                             xn[:, ic * 1024 + th * 512: ic * 1024 + (th + 1) * 512],
                             ic == 0, ic == 15)
                    t_ = PE.emit(fn) if ic == 15 else PE.emit(fn, inc=False)
                tgm[th] = t_
                if bankf[bu]:
                    PE.wait(*bankf[bu])
                t_ = None
                for ic in range(16):
                    fn = MMf(psf[bu][:, :], wgu[gsl % 2][:, 2048 + ic * 128: 2048 + (ic + 1) * 128],
                             xn[:, ic * 1024 + th * 512: ic * 1024 + (th + 1) * 512],
                             ic == 0, ic == 15)
                    t_ = PE.emit(fn) if ic == 15 else PE.emit(fn, inc=False)
                tum[th] = t_
            wgu_last[gsl % 2] = PE.tick
            if gsl + 2 < 64:
                load_wgu(gsl + 2)
            for th in range(2):
                bg = th
                bu = 2 + th
                ACT.wait(PE, tgm[th])
                if silu_free:
                    ACT.wait(DVE, silu_free)
                tsl_ = ACT.emit(lambda e, bg=bg, th=th: e.activation(
                    silu_sb[:, th * 512:(th + 1) * 512], psf[bg][:, :], AF.Silu))
                bankf[bg] = (ACT, tsl_)
                DVE.wait(ACT, tsl_)
                DVE.wait(PE, tum[th])
                if act_war:
                    DVE.wait(*act_war)
                    DVE.wait(*act_dve_war)
                tac = DVE.emit(lambda e, bu=bu, sl=sl, th=th: e.tensor_tensor(
                    act[:, sl * 1024 + th * 512: sl * 1024 + (th + 1) * 512],
                    psf[bu][:, :], silu_sb[:, th * 512:(th + 1) * 512], ALU.mult))
                bankf[bu] = (DVE, tac)
                silu_free = tac
                act_t[(sl, th)] = tac
            if r == 0 and sl == 0:
                tap_now("slu0", silu_sb)
        act_war = None
        act_ready = max(act_t.values())
        if r == 0:
            tap_now("act0", act)
        if r == 0:
            load_wd(0)
            load_wd(1)
        for dch in range(4):
            gdc = r * 4 + dch
            for tc in range(8):
                bd_ = 4 + tc % 2
                PE.wait(GPD, wd_tick[gdc])
                if bankf[bd_]:
                    PE.wait(*bankf[bd_])
                PE.wait(DVE, act_ready)
                tdm = None
                for hc in range(16):
                    fn = MMf(psf[bd_][:, :],
                             act[:, hc * 1024 + tc * 128: hc * 1024 + (tc + 1) * 128],
                             wd[gdc % 2][:, hc * 512:(hc + 1) * 512], hc == 0, hc == 15)
                    tdm = PE.emit(fn) if hc == 15 else PE.emit(fn, inc=False)
                DVE.wait(PE, tdm)
                DVE.wait(DVE, res_last[tc])
                tra = DVE.emit(lambda e, bd_=bd_, tc=tc, dch=dch: e.tensor_tensor(
                    res[:, tc * 2048 + dch * 512: tc * 2048 + (dch + 1) * 512],
                    psf[bd_][:, :],
                    res[:, tc * 2048 + dch * 512: tc * 2048 + (dch + 1) * 512],
                    ALU.add))
                bankf[bd_] = (DVE, tra)
                res_last[tc] = tra
            wd_last[gdc % 2] = PE.tick
            if gdc + 2 < 16:
                load_wd(gdc + 2)

    tstf = None
    for tc in range(8):
        SYD.wait(DVE, res_last[tc])
        tstf = SYD.emit(lambda e, tc=tc: e.dma_start(
            OUTd[tc * 128:(tc + 1) * 128, :], res[:, tc * 2048:(tc + 1) * 2048]))
    tstf = max(tstf, tap_store[0])
    SYD.wait(SYD, tstf)

    with nc.Block() as block:
        block.gpsimd(lambda g: GPD.replay(g))
        block.sync(lambda s: SYD.replay(s))
        block.tensor(lambda t: PE.replay(t))
        block.vector(lambda v: DVE.replay(v))
        block.scalar(lambda s: ACT.replay(s))
    return nc



def _pack_stream(qkv_w, gate_w, out_w, w_gate, w_up, w_down):
    idx = np.arange(DM)
    qrows = (idx // 64) * 192 + idx % 64
    WqT = np.ascontiguousarray(qkv_w[qrows].T.astype(np.float32))
    WkT = np.ascontiguousarray(qkv_w[qrows + 64].T.astype(np.float32))
    WvT = np.ascontiguousarray(qkv_w[qrows + 128].T.astype(np.float32))
    GWT = np.ascontiguousarray(gate_w.T.astype(np.float32))
    OWT = np.ascontiguousarray(out_w.T.astype(np.float32))
    WGT = np.ascontiguousarray(w_gate.T.astype(np.float32))
    WUT = np.ascontiguousarray(w_up.T.astype(np.float32))
    WDT = np.ascontiguousarray(w_down.T.astype(np.float32))
    WS = np.empty((128, STREAM_TOT), dtype=np.float32)
    for key, off in OFF.items():
        kind = key[0]
        if kind == "PK":
            _, q, ic = key
            WS[:, off:off + 512] = WkT[ic * 128:(ic + 1) * 128, q * 512:(q + 1) * 512]
        elif kind == "PV":
            _, q, ic = key
            WS[:, off:off + 512] = WvT[ic * 128:(ic + 1) * 128, q * 512:(q + 1) * 512]
        elif kind == "Q":
            _, q, cs, ic = key
            WS[:, off:off + 128] = WqT[ic * 128:(ic + 1) * 128,
                                       q * 512 + cs * 128: q * 512 + (cs + 1) * 128]
        elif kind == "K":
            _, q, cs, ic = key
            WS[:, off:off + 128] = WkT[ic * 128:(ic + 1) * 128,
                                       q * 512 + cs * 128: q * 512 + (cs + 1) * 128]
        elif kind == "V":
            _, q, ic = key
            WS[:, off:off + 512] = WvT[ic * 128:(ic + 1) * 128, q * 512:(q + 1) * 512]
        elif kind == "gw":
            _, gq, ic = key
            WS[:, off:off + 512] = GWT[ic * 128:(ic + 1) * 128, gq * 512:(gq + 1) * 512]
        elif kind == "ow":
            _, gq, dch, gch = key
            WS[:, off:off + 512] = OWT[gq * 512 + gch * 128: gq * 512 + (gch + 1) * 128,
                                       dch * 512:(dch + 1) * 512]
        elif kind == "wg":
            _, r, sl, ic = key
            WS[:, off:off + 128] = WGT[ic * 128:(ic + 1) * 128,
                                       (r * 16 + sl) * 128:(r * 16 + sl + 1) * 128]
        elif kind == "wu":
            _, r, sl, ic = key
            WS[:, off:off + 128] = WUT[ic * 128:(ic + 1) * 128,
                                       (r * 16 + sl) * 128:(r * 16 + sl + 1) * 128]
        elif kind == "wd":
            _, r, dch, hc = key
            WS[:, off:off + 512] = WDT[(r * 16 + hc) * 128:(r * 16 + hc + 1) * 128,
                                       dch * 512:(dch + 1) * 512]
        else:
            raise KeyError(kind)
    return WS


def _aux_arrays(norm_w):
    slt = np.array(slopes(), dtype=np.float64)
    p = np.arange(128)
    MND = np.full((128, 512), NEG, dtype=np.float32)
    for jc in range(2):
        j = jc * 128 + p[:, None]
        i = np.arange(256)[None, :]
        MND[:, jc * 256:(jc + 1) * 256] = np.where(i >= j, (j - i).astype(np.float64), NEG
                                                   ).astype(np.float32)
    KDP = np.empty((128, 4096), dtype=np.float32)
    for jh in range(2):
        jpos = jh * 128 + p
        for c in range(16):
            for fh in range(2):
                h = 2 * c + fh
                col = jh * 2048 + c * 128 + fh * 64
                KDP[:, col:col + 64] = np.exp(-slt[h] * (BS - (jpos + 1.0))
                                              )[:, None].astype(np.float32)
    QDB = np.empty((128, 4096), dtype=np.float32)
    jj = np.arange(256)[None, :] + 1.0
    for cg in range(16):
        hrow = 2 * cg + p // 64
        QDB[:, cg * 256:(cg + 1) * 256] = np.exp(-slt[hrow][:, None] * jj).astype(np.float32)
    NW = np.ascontiguousarray(norm_w.astype(np.float32).reshape(16, 128).T)
    IDT = np.eye(128, dtype=np.float32)
    return MND, KDP, QDB, NW, IDT


_nc_cache = {}


def kernel(L0_attn_hidden_states, L0_attn_qkv_w, L0_attn_out_w, L0_attn_gate_w,
           L0_attn_norm_w, L0_ffn_w_gate, L0_ffn_w_up, L0_ffn_w_down):
    x = np.ascontiguousarray(np.asarray(L0_attn_hidden_states, dtype=np.float32))
    WS = _pack_stream(np.asarray(L0_attn_qkv_w, dtype=np.float32),
                      np.asarray(L0_attn_gate_w, dtype=np.float32),
                      np.asarray(L0_attn_out_w, dtype=np.float32),
                      np.asarray(L0_ffn_w_gate, dtype=np.float32),
                      np.asarray(L0_ffn_w_up, dtype=np.float32),
                      np.asarray(L0_ffn_w_down, dtype=np.float32))
    MND, KDP, QDB, NW, _ = _aux_arrays(np.asarray(L0_attn_norm_w, dtype=np.float32))
    zeros = np.zeros((TK, DM), dtype=np.float32)
    in_maps = []
    for c in range(8):
        b, half = c // 2, c % 2
        xm = np.ascontiguousarray(x[b, half * TK:(half + 1) * TK, :])
        xp = zeros if half == 0 else np.ascontiguousarray(x[b, 0:TK, :])
        in_maps.append({"X": xm, "XP": xp, "WS": WS, "MND": MND, "KDP": KDP,
                        "QDB": QDB, "NW": NW})
    if "nc" not in _nc_cache:
        _nc_cache["nc"] = build_program()
    nc = _nc_cache["nc"]
    r = run_bass_kernel_spmd(nc, in_maps, core_ids=list(range(8)))
    out = np.empty((4, 2048, DM), dtype=np.float32)
    for c in range(8):
        b, half = c // 2, c % 2
        out[b, half * TK:(half + 1) * TK, :] = r.results[c]["OUT"]
    return out
